# version 22
# baseline (speedup 1.0000x reference)
import numpy as np
import ml_dtypes
import concourse.bass as bass
import concourse.mybir as mybir
from concourse.bass_utils import run_bass_kernel_spmd
from contextlib import ExitStack

F32 = mybir.dt.float32
BF16 = mybir.dt.bfloat16
AF = mybir.ActivationFunctionType
ALU = mybir.AluOpType
AX = mybir.AxisListType

B, S, D = 4, 4096, 1024
NH, HD = 16, 64
NT = S // 128
OWN = NT // 2
NBLK_A = S // 512
NBLK_B = 2048 // 512
NGRP = 4
NBLK_D = 2048 // 512
DH = 4096
EPS = 1e-5
MASKB = -32768.0

ENGS = ("sync", "gpsimd", "tensor", "vector", "scalar")


def _build(ffn_bf16_flags):
    ffn_all_bf16 = all(ffn_bf16_flags.values())
    nc = bass.Bass("TRN2", target_bir_lowering=False)

    x_kv = nc.dram_tensor("x_kv", [S, D], F32, kind="ExternalInput").ap()
    x_q = nc.dram_tensor("x_q", [2048, D], F32, kind="ExternalInput").ap()
    wqk_d = nc.dram_tensor("wqk", [D, 2 * D], F32, kind="ExternalInput").ap()
    bqk_d = nc.dram_tensor("bqk", [128, 16], F32, kind="ExternalInput").ap()
    wv_d = nc.dram_tensor("wv", [D, D], F32, kind="ExternalInput").ap()
    bv_d = nc.dram_tensor("bv", [1, D], F32, kind="ExternalInput").ap()
    ident_d = nc.dram_tensor("ident", [128, 128], BF16, kind="ExternalInput").ap()
    identf_d = nc.dram_tensor("identf", [128, 128], F32, kind="ExternalInput").ap()
    hm_d = nc.dram_tensor("hm", [128, 128], BF16, kind="ExternalInput").ap()
    gm_d = nc.dram_tensor("gm", [8, 128, 512], BF16, kind="ExternalInput").ap()
    wspec = {"w1": 32, "w2": 32, "h1": (32, 32), "h2": (32, 32), "w3": (8, 32)}
    wdim = {"w1": (32, 8), "w2": (32, 8), "h1": (32, 32), "h2": (32, 32), "w3": (8, 32)}
    wdram = {}
    wmirror = {}
    for nm, (no, ni) in wdim.items():
        dt = BF16 if ffn_bf16_flags[nm] else F32
        wdram[nm] = nc.dram_tensor("w_" + nm, [no, 128, ni, 128], dt, kind="ExternalInput").ap()
        if not ffn_bf16_flags[nm]:
            wmirror[nm] = nc.dram_tensor("wm_" + nm, [no, 128, ni, 128], BF16, kind="Internal").ap()
    sc_d = {}
    for nm, w in (("s1", 32), ("s2", 32), ("sh1", 32), ("sh2", 32), ("s3", 8)):
        sc_d[nm] = nc.dram_tensor(nm, [128, w], F32, kind="ExternalInput").ap()
    k_sp = nc.dram_tensor("k_sp", [D, S], BF16, kind="Internal").ap()
    v_sp = nc.dram_tensor("v_sp", [S, 1040], BF16, kind="Internal").ap()
    x2_sp = nc.dram_tensor("x2_sp", [2048, D], F32, kind="Internal").ap()
    out_d = nc.dram_tensor("out", [2048, D], F32, kind="ExternalOutput").ap()

    es = ExitStack()
    sb = lambda name, shape, dt: es.enter_context(nc.sbuf_tensor(name, shape, dt))
    ident = sb("identb", [128, 128], BF16)
    identf = sb("identfb", [128, 128], F32)
    hm = sb("hmb", [128, 128], BF16)
    gm = sb("gmb", [128, 8, 512], BF16)
    ones_col = sb("ones_col", [1, 128], BF16)
    e0m = sb("e0m", [128, 128], BF16)
    ohm = sb("ohm", [128, 128], BF16)
    sel0 = sb("sel0", [2, 128], BF16)
    sel1 = sb("sel1", [2, 128], BF16)
    bv_pad = sb("bv_pad", [128, 1024], BF16)
    bqk_c = sb("bqk_c", [128, 16], F32)
    scc = {nm: sb("c_" + nm, [128, w], F32) for nm, w in
           (("s1", 32), ("s2", 32), ("sh1", 32), ("sh2", 32), ("s3", 8))}
    kn_part = sb("kn_part", [2, 64], F32)
    qn_part = sb("qn_part", [2, 32], F32)
    kmax = sb("kmax", [2, 8], F32)
    m2 = sb("m2", [2, 32], F32)
    msm = sb("msm", [2, 32], BF16)
    mb = sb("mb", [128, 64], F32)
    xbuf = [sb(f"xbuf{i}", [128, 4, 1024], F32) for i in range(3)]
    stats = [sb(f"stats{i}", [128, 48], F32) for i in range(2)]
    mv = [sb(f"mv{i}", [128, 8], F32) for i in range(2)]
    tmp4 = [sb(f"tmp4_{i}", [128, 4], F32) for i in range(2)]
    rec4 = [sb(f"rec4_{i}", [128, 4], F32) for i in range(2)]
    rstd4 = [sb(f"rstd4_{i}", [128, 4], F32) for i in range(2)]
    nb4 = [sb(f"nb4_{i}", [128, 4], F32) for i in range(2)]
    mbp = sb("mbp", [128, 32], F32)

    ao = [sb(f"ao{i}", [65, 512], F32) for i in range(2)]
    zc = sb("zc", [128, 4], F32)
    zr = sb("zr", [128, 4], F32)

    es_ps = ExitStack()
    pf = [es_ps.enter_context(nc.psum_tensor(f"pf{i}", [128, 512], F32)) for i in range(6)]
    pb = [es_ps.enter_context(nc.psum_tensor(f"pb{i}", [128, 1024], BF16)) for i in range(2)]

    semnames = (["dc", "dg", "sg", "sh", "sw"] + [f"d{i}" for i in range(8)] + [f"g{i}" for i in range(8)]
                + ["gm0", "pe", "dv", "ac", "ph"])
    sem = {n: es.enter_context(nc.semaphore("s_" + n)) for n in semnames}
    C = {n: 0 for n in sem}

    EV = {e: [] for e in ENGS}

    def on(e, meth, *args, inc=None, n=None, **kw):
        def f(g, meth=meth, args=args, kw=kw, inc=inc, n=n):
            ins = getattr(g, meth)(*args, **kw)
            if inc is not None:
                ins.then_inc(sem[inc], n)
        EV[e].append(f)
        if inc is not None:
            C[inc] += n

    def wait(e, s, v):
        if v > 0:
            EV[e].append(lambda g, s=s, v=v: g.wait_ge(sem[s], v))

    def dma(e, out, in_, inc):
        on(e, "dma_start", out=out, in_=in_, inc=inc, n=16)

    def barrier():
        for e in ENGS:
            on(e, "nop", inc="ph", n=1)
        tgt = C["ph"]
        for e in ENGS:
            wait(e, "ph", tgt)

    if not ffn_all_bf16:
        pass
    else:
        dma("sync", xbuf[0][:], x_kv[0:512, :].rearrange("(t p) d -> p t d", p=128), "d0")
        dma("sync", xbuf[1][:], x_kv[512:1024, :].rearrange("(t p) d -> p t d", p=128), "d1")
    dma("sync", ident[:], ident_d, "dc")
    dma("sync", identf[:], identf_d, "dc")
    dma("sync", hm[:], hm_d, "dc")
    dma("sync", bqk_c[:], bqk_d, "dc")
    for nm in scc:
        dma("sync", scc[nm][:], sc_d[nm], "dc")
    es_q = ExitStack()
    q_fm = es_q.enter_context(nc.sbuf_tensor("q_fm", [128, 8, 2048], BF16))
    bv_f = es_q.enter_context(nc.sbuf_tensor("bv_f", [1, D], F32))
    dma("sync", bv_f[:], bv_d, "dc")
    dc_total = C["dc"]
    wait("vector", "dc", dc_total)
    on("vector", "memset", ones_col[:], 1.0)
    on("vector", "memset", e0m[:], 0.0)
    on("vector", "memset", ohm[:], 0.0)
    on("vector", "memset", sel0[:], 0.0)
    on("vector", "memset", sel1[:], 1.0)
    on("vector", "memset", bv_pad[:], 0.0)
    on("vector", "drain")
    on("vector", "memset", e0m[0:1, :], 1.0)
    on("vector", "memset", ohm[0:64, 0:1], 1.0)
    on("vector", "memset", ohm[64:128, 1:2], 1.0)
    on("vector", "memset", sel0[0:1, :], 1.0)
    on("vector", "memset", sel1[0:1, :], 0.0)
    on("vector", "tensor_copy", bv_pad[0:1, :], bv_f[:], inc="dv", n=1)
    es_ab = ExitStack()
    sab = lambda name, shape, dt: es_ab.enter_context(nc.sbuf_tensor(name, shape, dt))
    wqk_bf = sab("wqk_bf", [128, 8, 2048], BF16)
    wv_bf = sab("wv_bf", [128, 8, 1024], BF16)
    xf = xbuf[0][:].rearrange("p a d -> p (a d)")
    xf2 = xbuf[1][:].rearrange("p a d -> p (a d)")
    xf3 = xbuf[2][:].rearrange("p a d -> p (a d)")
    pre_snap = {0: 16, 1: 16}
    dma("sync", gm[:], gm_d.rearrange("r p q -> p r q"), "dg")
    gm_snap = C["dg"]
    stage_jobs = []
    for ic in range(8):
        stage_jobs.append((wqk_d.rearrange("(i p) o -> p i o", p=128)[:, ic:ic + 1, 1024:2048],
                           wqk_bf[:, ic:ic + 1, 1024:2048], 1024, 1024))
    for half in range(4):
        stage_jobs.append((wv_d.rearrange("(i p) o -> p i o", p=128)[:, 2 * half:2 * half + 2, :],
                           wv_bf[:, 2 * half:2 * half + 2, :], 1024, 2048))
    for ic in range(8):
        stage_jobs.append((wqk_d.rearrange("(i p) o -> p i o", p=128)[:, ic:ic + 1, 0:1024],
                           wqk_bf[:, ic:ic + 1, 0:1024], 1024, 1024))
    for j, (src_ap, dst_ap, owid, nel) in enumerate(stage_jobs):
        hv = xf3[:, (j % 2) * 2048:(j % 2) * 2048 + nel].rearrange("p (i o) -> p i o", o=owid)
        ssem = "d2" if j % 2 == 0 else "sh"
        if j >= 2:
            wait("sync", "sg", j - 1)
        dma("sync", hv, src_ap, ssem)
        wait("gpsimd", ssem, C[ssem])
        on("gpsimd", "tensor_copy", dst_ap, hv, inc="sg", n=1)
    stage_ph = C["sg"]
    stage_k, stage_v, stage_q = stage_ph - 12, stage_ph - 8, stage_ph

    if wmirror:
        wait("sync", "sg", C["sg"])
    for nm in wmirror:
        no, ni = wdim[nm]
        for oc in range(no):
            wait("sync", "sw", C["sw"])
            dma("sync", xf3.rearrange("p (i o) -> p i o", o=128)[:, 0:ni, :],
                wdram[nm][oc], "d2")
            wait("vector", "d2", C["d2"])
            wait("vector", "gm0", C["gm0"])
            on("vector", "tensor_copy", xf2[:, 0:ni * 128], xf3[:, 0:ni * 128], inc="sw", n=1)
            wait("gpsimd", "sw", C["sw"])
            dma("gpsimd", wmirror[nm][oc],
                xf2.rearrange("p (i o) -> p i o", o=128)[:, 0:ni, :], "gm0")
    wsrc = {nm: (wmirror[nm] if nm in wmirror else wdram[nm]) for nm in wdim}
    phasew_gm = C["gm0"]

    xn_tm = [sab(f"xn_tm{i}", [128, 4, 1024], BF16) for i in range(2)]
    xn_fm = [sab(f"xn_fm{i}", [128, 8, 512], BF16) for i in range(2)]
    kst = [sab(f"kst{i}", [128, 512], BF16) for i in range(2)]
    ksq = [sab(f"ksq{i}", [128, 512], BF16) for i in range(2)]
    vstage = [sab(f"vstage{i}", [128, 1040], BF16) for i in range(2)]

    for i in range(2):
        on("vector", "memset", vstage[i][:].rearrange("p (h c) -> p h c", c=65)[:, :, 64:65], 1.0)
    on("vector", "nop", inc="dv", n=1)
    wait("tensor", "dc", dc_total)
    wait("scalar", "dc", dc_total)
    if wmirror:
        wait("tensor", "sg", stage_ph)
        barrier()
        wait("sync", "gm0", C["gm0"])
        dma("sync", xbuf[0][:], x_kv[0:512, :].rearrange("(t p) d -> p t d", p=128), "d0")
        pre_snap[0] = C["d0"]
        dma("sync", xbuf[1][:], x_kv[512:1024, :].rearrange("(t p) d -> p t d", p=128), "d1")
        pre_snap[1] = C["d1"]

    ln_calls = [0]
    pb_last = {0: 0, 1: 0}
    pending_sqrt = [None]
    ln_snb = [0]

    def flush_sqrt():
        if pending_sqrt[0] is not None:
            pending_sqrt[0]()
            pending_sqrt[0] = None

    def emit_ln_stats(xb, xsem, xdma_snap, p):
        st, mvp, tm, rc = stats[p], mv[p], tmp4[p], rec4[p]
        wait("vector", xsem, xdma_snap)
        on("vector", "drain")
        for t in range(4):
            on("vector", "bn_stats", st[:, t * 12:t * 12 + 6], xb[:, t, 0:512])
            on("vector", "bn_stats", st[:, t * 12 + 6:t * 12 + 12], xb[:, t, 512:1024])
        on("vector", "drain")
        for t in range(4):
            on("vector", "bn_aggr", mvp[:, 2 * t:2 * t + 2], st[:, t * 12:t * 12 + 12])
        on("vector", "drain")
        on("vector", "tensor_scalar_add", tm[:], mvp[:].rearrange("p (t s) -> p t s", s=2)[:, :, 1], EPS)
        on("vector", "drain")
        on("vector", "reciprocal", rc[:], tm[:], inc="dv", n=1)
        snap_dv = C["dv"]

        def ln_sqrt_part(snap_dv=snap_dv, p=p):
            wait("scalar", "dv", snap_dv)
            on("scalar", "drain")
            on("scalar", "activation", rstd4[p][:], rec4[p][:], AF.Sqrt, inc="ac", n=1)
            on("scalar", "drain")
            snap_ac = C["ac"]
            wait("vector", "ac", snap_ac)
            on("vector", "scalar_tensor_tensor", nb4[p][:], mv[p][:].rearrange("p (t s) -> p t s", s=2)[:, :, 0], -1.0, rstd4[p][:], ALU.mult, ALU.mult, inc="dv", n=1)
            ln_snb[0] = C["dv"]
        pending_sqrt[0] = ln_sqrt_part
        return None

    def emit_ln_finish(xb, xnt, xn, p):
        flush_sqrt()
        wait("scalar", "dv", ln_snb[0])
        for t in range(4):
            on("scalar", "activation", xnt[:, t, :], xb[:, t, :], AF.Identity,
               scale=rstd4[p][:, t:t + 1], bias=nb4[p][:, t:t + 1],
               inc=("ac" if t == 3 else None), n=(1 if t == 3 else None))
        snap_apply = C["ac"]
        wait("tensor", "ac", snap_apply)
        for t in range(4):
            for half in range(2):
                bank = pb[half]
                wait("tensor", "dv", pb_last[half])
                for c4 in range(4):
                    c = half * 4 + c4
                    on("tensor", "transpose", bank[:, c4 * 128:(c4 + 1) * 128],
                       xnt[:, t, c * 128:(c + 1) * 128], ident[:],
                       inc=("pe" if c4 == 3 else None), n=(1 if c4 == 3 else None))
                snap_tr = C["pe"]
                wait("vector", "pe", snap_tr)
                on("vector", "tensor_copy",
                   xn[:, half * 4:half * 4 + 4, t * 128:(t + 1) * 128],
                   bank[:, 0:512].rearrange("p (c q) -> p c q", c=4),
                   inc="dv", n=1)
                pb_last[half] = C["dv"]
        return C["dv"], snap_apply

    xa_lastuse = {}
    xn_ready = {}
    kst_spill = {}
    ksq_use = {}
    vstage_spill = {}
    vstage_ready = {}
    kn_red = {}
    qn_red = {}
    xn_lastuse = {}

    def emit_kv_body(a, xn, xnrdy):
        wait("tensor", "dv", xnrdy)
        if a == 0:
            wait("tensor", "sg", stage_k)
        for kc in range(8):
            bank = pf[kc % 2]
            if kc >= 2:
                wait("tensor", "ac", ksq_use.get(("evac", a, kc - 2), 0))
            for ic in range(8):
                on("tensor", "matmul", bank[:],
                   wqk_bf[:, ic, 1024 + kc * 128:1024 + (kc + 1) * 128],
                   xn[:, ic, :], start=(ic == 0), stop=(ic == 7),
                   inc=("pe" if ic == 7 else None), n=(1 if ic == 7 else None))
            snap_mm = C["pe"]
            wait("scalar", "pe", snap_mm)
            idx = a * 8 + kc
            if idx >= 2:
                wait("scalar", f"g{kc % 2}", kst_spill[idx - 2])
            on("scalar", "activation", kst[kc % 2][:], bank[:], AF.Identity,
               bias=bqk_c[:, 8 + kc:9 + kc], inc="ac", n=1)
            if idx >= 2:
                wait("scalar", "pe", ksq_use.get(("nmm", a, kc - 2), 0))
            on("scalar", "activation", ksq[kc % 2][:], bank[:], AF.Square,
               bias=bqk_c[:, 8 + kc:9 + kc], inc="ac", n=1)
            ksq_use[("evac", a, kc)] = C["ac"]
            snap_evac = C["ac"]
            wait("gpsimd", "ac", snap_evac - 1)
            dma("gpsimd", k_sp[kc * 128:(kc + 1) * 128, a * 512:(a + 1) * 512], kst[kc % 2][:], f"g{kc % 2}")
            kst_spill[idx] = C[f"g{kc % 2}"]

            def norm_mm(kcn, idxn):
                wait("tensor", "ac", ksq_use[("evac", a, kcn)])
                if idxn >= 1:
                    wait("tensor", "dv", kn_red[idxn - 1])
                on("tensor", "matmul", pf[2][:], ohm[:], ksq[kcn % 2][:], start=True, stop=True, inc="pe", n=1)
                ksq_use[("nmm", a, kcn)] = C["pe"]
                wait("vector", "pe", C["pe"])
                on("vector", "tensor_reduce", kn_part[0:2, kcn * 8 + a: kcn * 8 + a + 1], pf[2][0:2, :], AX.X, ALU.max, inc="dv", n=1)
                kn_red[idxn] = C["dv"]
            if kc == 3:
                flush_sqrt()
            if kc >= 1:
                norm_mm(kc - 1, idx - 1)
            if kc == 7:
                norm_mm(7, idx)
        if a == 0:
            wait("tensor", "sg", stage_v)
        for t in range(4):
            banks = (pf[4], pf[5]) if t % 2 == 0 else (pf[0], pf[1])
            vidx = a * 4 + t
            if t % 2 == 1:
                wait("tensor", "ac", ksq_use[("evac", a, 7)])
            if vidx >= 2:
                wait("tensor", "dv", vstage_ready[vidx - 2])
            for half in range(2):
                bank = banks[half]
                on("tensor", "matmul", bank[:], e0m[:], bv_pad[:, half * 512:(half + 1) * 512], start=True, stop=False)
                for ic in range(8):
                    on("tensor", "matmul", bank[:],
                       xn[:, ic, t * 128:(t + 1) * 128],
                       wv_bf[:, ic, half * 512:(half + 1) * 512],
                       start=False, stop=(ic == 7),
                       inc=("pe" if ic == 7 else None), n=(1 if ic == 7 else None))
            snap_vmm = C["pe"]
            wait("vector", "pe", snap_vmm)
            if vidx >= 2:
                wait("vector", f"g{2 + t % 2}", vstage_spill[vidx - 2])
            for half in range(2):
                on("vector", "tensor_copy",
                   vstage[t % 2][:].rearrange("p (h c) -> p h c", c=65)[:, half * 8:half * 8 + 8, 0:64],
                   banks[half][:].rearrange("p (h d) -> p h d", d=64),
                   inc=("dv" if half == 1 else None), n=(1 if half == 1 else None))
            vstage_ready[vidx] = C["dv"]
            wait("gpsimd", "dv", vstage_ready[vidx])
            dma("gpsimd", v_sp[a * 512 + t * 128: a * 512 + (t + 1) * 128, :], vstage[t % 2][:], f"g{2 + t % 2}")
            vstage_spill[vidx] = C[f"g{2 + t % 2}"]

    def emit_q_body(b, xn, xnrdy):
        wait("tensor", "dv", xnrdy)
        if b == 0:
            wait("tensor", "sg", stage_q)
        for qc in range(8):
            bank = pf[qc % 2]
            idx = b * 8 + qc
            if qc >= 2:
                wait("tensor", "ac", ksq_use.get(("qevac", b, qc - 2), 0))
            elif b == 0 and qc < 2:
                wait("tensor", "dv", vstage_ready[NBLK_A * 4 - 1])
            for ic in range(8):
                on("tensor", "matmul", bank[:],
                   wqk_bf[:, ic, qc * 128:(qc + 1) * 128],
                   xn[:, ic, :], start=(ic == 0), stop=(ic == 7),
                   inc=("pe" if ic == 7 else None), n=(1 if ic == 7 else None))
            snap_mm = C["pe"]
            wait("scalar", "pe", snap_mm)
            on("scalar", "activation", q_fm[:, qc, b * 512:(b + 1) * 512], bank[:],
               AF.Identity, bias=bqk_c[:, qc:qc + 1], inc="ac", n=1)
            if idx >= 2:
                wait("scalar", "pe", ksq_use.get(("qnmm", b, qc - 2), 0))
            on("scalar", "activation", ksq[qc % 2][:], bank[:], AF.Square,
               bias=bqk_c[:, qc:qc + 1], inc="ac", n=1)
            ksq_use[("qevac", b, qc)] = C["ac"]

            def qnorm_mm(qcn, idxn):
                wait("tensor", "ac", ksq_use[("qevac", b, qcn)])
                if idxn >= 1:
                    wait("tensor", "dv", qn_red[idxn - 1])
                on("tensor", "matmul", pf[2][:], ohm[:], ksq[qcn % 2][:], start=True, stop=True, inc="pe", n=1)
                ksq_use[("qnmm", b, qcn)] = C["pe"]
                wait("vector", "pe", C["pe"])
                on("vector", "tensor_reduce", qn_part[0:2, qcn * 4 + b: qcn * 4 + b + 1], pf[2][0:2, :], AX.X, ALU.max, inc="dv", n=1)
                qn_red[idxn] = C["dv"]
            if qc == 3:
                flush_sqrt()
            if qc >= 1:
                qnorm_mm(qc - 1, idx - 1)
            if qc == 7:
                qnorm_mm(7, idx)

    NALL = NBLK_A + NBLK_B
    binfo = [("A", i) for i in range(NBLK_A)] + [("B", i) for i in range(NBLK_B)]
    snb = {}
    for i in range(NALL + 1):
        if i < NALL:
            ph, idx = binfo[i]
            xb = xbuf[i % 2]
            if i >= 2:
                wait("sync", "ac", xa_lastuse[i - 2])
                src_ap = (x_kv[idx * 512:(idx + 1) * 512, :] if ph == "A"
                          else x_q[idx * 512:(idx + 1) * 512, :])
                dma("sync", xb[:], src_ap.rearrange("(t p) d -> p t d", p=128), f"d{i % 2}")
                snap_x = C[f"d{i % 2}"]
            else:
                snap_x = pre_snap[i]
            snb[i] = emit_ln_stats(xb, f"d{i % 2}", snap_x, i % 2)
        if i >= 1:
            ph1, idx1 = binfo[i - 1]
            if ph1 == "A":
                emit_kv_body(idx1, xn_fm[(i - 1) % 2], xn_ready[i - 1])
            else:
                emit_q_body(idx1, xn_fm[(i - 1) % 2], xn_ready[i - 1])
        if i < NALL:
            xn_ready[i], xa_lastuse[i] = emit_ln_finish(xbuf[i % 2], xn_tm[i % 2], xn_fm[i % 2], i % 2)
    barrier()
    es_ab.close()
    es_ps.close()

    es_c = ExitStack()
    scb = lambda name, shape, dt: es_c.enter_context(nc.sbuf_tensor(name, shape, dt))
    psc = [es_c.enter_context(nc.psum_tensor(f"psc{i}", [128, 1024], F32)) for i in range(2)]
    pav = [es_c.enter_context(nc.psum_tensor(f"pav{i}", [128, 512], F32)) for i in range(4)]
    vb = [scb(f"vb{i}", [128, 32, 130], BF16) for i in range(3)]
    kb = [scb(f"kb{i}", [128, 4096], BF16) for i in range(3)]
    ptb = scb("ptb", [128, 8, 1024], BF16)

    for gs in ("g0", "g1", "g2", "g3"):
        wait("scalar", gs, C[gs])
    wait("tensor", "dg", gm_snap)
    on("vector", "tensor_reduce", kmax[:], kn_part[:].rearrange("p (h a) -> p h a", a=8), AX.X, ALU.max)
    on("vector", "drain")
    for g_ in range(NGRP):
        on("vector", "tensor_mul", m2[:, g_ * 8:(g_ + 1) * 8],
           qn_part[:].rearrange("p (h g) -> p h g", g=4)[:, :, g_], kmax[:])
    on("vector", "drain")
    on("vector", "nop", inc="dv", n=1)
    wait("scalar", "dv", C["dv"])
    on("scalar", "activation", msm[:], m2[:], AF.Sqrt, scale=1.0 / 64.0, inc="ac", n=1)
    wait("tensor", "ac", C["ac"])
    on("tensor", "matmul", psc[0][:, 0:32], sel0[:], msm[:], start=True, stop=True)
    on("tensor", "matmul", psc[0][:, 32:64], sel1[:], msm[:], start=True, stop=True, inc="pe", n=1)
    wait("scalar", "pe", C["pe"])
    on("scalar", "activation", mb[:], psc[0][:, 0:64], AF.Identity, scale=-1.0, inc="ac", n=1)
    on("scalar", "drain")
    snapm = C["ac"]
    wait("vector", "ac", snapm)
    on("vector", "tensor_tensor", mbp[:], mb[:, 0:32], mb[:, 32:64], ALU.min, inc="dv", n=1)
    wait("scalar", "dv", C["dv"])
    mb_ready = C["ac"]

    kb_lastuse = {}
    vb_lastuse = {}
    pending_tail = [None]
    last_ao = [0]
    kdma_snap = {}
    vdma_snap = {}
    av_snap = {}
    KSEM = ["d4", "d5", "d2"]
    VSEM = ["d6", "d7", "d3"]
    semA_last = {0: 0, 1: 0, 2: 0}
    semB_last = {0: 0, 1: 0, 2: 0}

    def emit_kv_load(n):
        if n >= NGRP * 8:
            return
        gl, hl = n // 8, n % 8
        El = 8 * gl + 8
        KEl = El * 128
        wait("sync", "pe", semA_last[n % 3])
        wait("sync", f"g{hl % 2}", kst_spill[(2 * gl + 1) * 8 + hl])
        dma("sync", kb[n % 3][:, 0:KEl], k_sp[hl * 128:(hl + 1) * 128, 0:KEl], KSEM[n % 3])
        kdma_snap[n] = C[KSEM[n % 3]]
        wait("sync", "pe", semB_last[n % 3])
        wait("sync", "g2", vstage_spill[(2 * gl + 1) * 4 + 2])
        wait("sync", "g3", vstage_spill[(2 * gl + 1) * 4 + 3])
        dma("sync", vb[n % 3][:, 0:El, :], v_sp[0:KEl, 130 * hl:130 * (hl + 1)].rearrange("(e p) c -> p e c", p=128), VSEM[n % 3])
        vdma_snap[n] = C[VSEM[n % 3]]
    exp_done = {}
    av_done = {}
    stt_done = {}
    x2b_spill = {}

    emit_kv_load(0)
    emit_kv_load(1)
    snb_d = {}
    xn_rdy_d = {}
    out_dma = {}

    def emit_d_load_stats(b):
        xb = xbuf[(b + 1) % 2]
        if b == 0:
            wait("sync", "g4", 32)
        if b == 1:
            wait("sync", "dv", stt_done[(3, 7, 1)])
        wait("sync", f"g{4 + b % 2}", 16 * (b // 2 + 1))
        if b >= 2:
            wait("sync", f"g{6 + b % 2}", out_dma[b - 2])
        dma("sync", xb[:], x2_sp[b * 512:(b + 1) * 512, :].rearrange("(t p) d -> p t d", p=128), f"d{(b + 1) % 2}")
        snap_x = C[f"d{(b + 1) % 2}"]
        snb_d[b] = emit_ln_stats(xb, f"d{(b + 1) % 2}", snap_x, b % 2)

    for g in range(NGRP):
        if g == 3:
            emit_d_load_stats(0)
            flush_sqrt()
        E = 8 * g + 8
        KE = E * 128
        xq = xbuf[0]
        if g >= 1:
            wait("sync", "dv", stt_done[(g - 1, 7, 1)])
        dma("sync", xq[:], x_q[g * 512:(g + 1) * 512, :].rearrange("(t p) d -> p t d", p=128), "d0")
        xq_snap = C["d0"]
        x2b = xbuf[1 + g % 2]
        if g >= 2:
            wait("vector", f"g{4 + g % 2}", x2b_spill[g - 2])
        for hp in range(8):
            h0, h1 = 2 * hp, 2 * hp + 1
            if (g, hp) != (0, 0):
                pass
            n_lin = g * 8 + hp
            emit_kv_load(n_lin + 2)
            kdma = kdma_snap[n_lin]
            vdma = vdma_snap[n_lin]
            wait("tensor", KSEM[(g * 8 + hp) % 3], kdma)
            wait("tensor", "ac", mb_ready)
            if g >= 1 or hp >= 1:
                prevpair = (g, hp - 1) if hp >= 1 else (g - 1, 7)
                wait("scalar", "pe", av_done.get(prevpair, 0))
            prevpair = (g, hp - 1) if hp >= 1 else (g - 1, 7)
            wait("tensor", VSEM[(g * 8 + hp) % 3], vdma)

            def av_mms(c, stop):
                on("tensor", "matmul", pav[0][0:65, :], vb[(g * 8 + hp) % 3][:, c, 0:65], ptb[:, c % 8, 0:512], start=(c == 0), stop=stop)
                on("tensor", "matmul", pav[1][0:65, :], vb[(g * 8 + hp) % 3][:, c, 65:130], ptb[:, c % 8, 512:1024], start=(c == 0), stop=stop,
                   inc="pe", n=1)
                av_snap[(g, hp, c)] = C["pe"]

            for c in range(E):
                bank = psc[c % 2]
                bA = bank[:, 0:512]
                bB = bank[:, 512:1024]
                if c >= 2:
                    wait("tensor", "ac", exp_done[(g, hp, c - 2)])
                elif g > 0 or hp > 0:
                    wait("tensor", "ac", exp_done[(prevpair[0], prevpair[1], 8 * prevpair[0] + 8 - 1 - (1 - c))])
                masked = c >= 8 * g
                on("tensor", "matmul", bA, kb[(g * 8 + hp) % 3][0:64, c * 128:(c + 1) * 128],
                   q_fm[0:64, hp, g * 512:(g + 1) * 512], start=True, stop=not masked)
                on("tensor", "matmul", bB, kb[(g * 8 + hp) % 3][64:128, c * 128:(c + 1) * 128],
                   q_fm[64:128, hp, g * 512:(g + 1) * 512], start=True, stop=not masked,
                   inc=(None if masked else "pe"), n=(None if masked else 1))
                if masked:
                    rel = c - 8 * g
                    on("tensor", "matmul", bA, hm[:], gm[:, rel, :], start=False, stop=True)
                    on("tensor", "matmul", bB, hm[:], gm[:, rel, :], start=False, stop=True, inc="pe", n=1)
                snap_s = C["pe"]
                wait("scalar", "pe", snap_s)
                if c >= 8:
                    wait("scalar", "pe", av_snap[(g, hp, c - 8)])
                on("scalar", "activation", ptb[:, c % 8, :], bank[:], AF.Exp, scale=0.125,
                   bias=mbp[:, g * 8 + hp:g * 8 + hp + 1], inc="ac", n=1)
                exp_done[(g, hp, c)] = C["ac"]
                if c == 3 and pending_tail[0] is not None:
                    pending_tail[0]()
                    pending_tail[0] = None
                if c >= 2:
                    if c == 2 and (g > 0 or hp > 0):
                        wait("tensor", "dv", last_ao[0])
                    wait("tensor", "ac", exp_done[(g, hp, c - 2)])
                    av_mms(c - 2, False)
            wait("tensor", "ac", exp_done[(g, hp, E - 2)])
            av_mms(E - 2, False)
            wait("tensor", "ac", exp_done[(g, hp, E - 1)])
            av_mms(E - 1, True)
            av_done[(g, hp)] = C["pe"]
            semA_last[(g * 8 + hp) % 3] = C["pe"]
            semB_last[(g * 8 + hp) % 3] = C["pe"]
            wait("vector", "pe", av_done[(g, hp)])
            for pi in range(2):
                on("vector", "tensor_copy", ao[pi][:], pav[pi][0:65, :],
                   inc=("dv" if pi == 1 else None), n=(1 if pi == 1 else None))
            ao_snap = C["dv"]

            def attn_tail(g=g, hp=hp, ao_snap=ao_snap, xq=xq, x2b=x2b, xq_snap=xq_snap):
                for pi in range(2):
                    h = 2 * hp + pi
                    psT = pav[2 + pi]
                    wait("tensor", "dv", ao_snap)
                    if (g, hp) != (0, 0):
                        pv = (g, hp - 1) if hp >= 1 else (g - 1, 7)
                        wait("tensor", "dv", stt_done.get((pv[0], pv[1], pi), 0))
                    for t in range(4):
                        on("tensor", "transpose", psT[:, t * 65:t * 65 + 65], ao[pi][0:65, t * 128:(t + 1) * 128], identf[0:65, 0:65],
                           inc=("pe" if t == 3 else None), n=(1 if t == 3 else None))
                    snap_tr = C["pe"]
                    wait("vector", "pe", snap_tr)
                    wait("vector", "d0", xq_snap)
                    on("vector", "tensor_copy", zc[:], psT[:, 0:260].rearrange("p (t c) -> p t c", c=65)[:, :, 64])
                    on("vector", "drain")
                    on("vector", "reciprocal", zr[:], zc[:])
                    on("vector", "drain")
                    for t in range(4):
                        on("vector", "scalar_tensor_tensor", x2b[:, t, h * 64:(h + 1) * 64],
                           psT[:, t * 65:t * 65 + 64], zr[:, t:t + 1], xq[:, t, h * 64:(h + 1) * 64],
                           ALU.mult, ALU.add, inc=("dv" if t == 3 else None), n=(1 if t == 3 else None))
                    stt_done[(g, hp, pi)] = C["dv"]
            pending_tail[0] = attn_tail
            last_ao[0] = ao_snap
        if pending_tail[0] is not None:
            pending_tail[0]()
            pending_tail[0] = None
        wait("gpsimd", "dv", stt_done[(g, 7, 1)])
        dma("gpsimd", x2_sp[g * 512:(g + 1) * 512, :].rearrange("(t p) d -> p t d", p=128), x2b[:], f"g{4 + g % 2}")
        x2b_spill[g] = C[f"g{4 + g % 2}"]
    barrier()
    es_c.close()
    es_q.close()

    es_d = ExitStack()
    sdb = lambda name, shape, dt: es_d.enter_context(nc.sbuf_tensor(name, shape, dt))
    xn2_tm = [sdb(f"xn2tm{i}", [128, 4, 1024], BF16) for i in range(2)]
    xn2_fm = [sdb(f"xn2fm{i}", [128, 8, 512], BF16) for i in range(2)]
    wb12 = [sdb(f"wb12_{i}", [128, 2, 8, 128], BF16) for i in range(3)]
    wbh = [sdb(f"wbh{i}", [128, 32, 128], BF16) for i in range(3)]
    su = [sdb(f"su{i}", [128, 512], BF16) for i in range(2)]
    sv2 = [sdb(f"sv2_{i}", [128, 512], BF16) for i in range(2)]
    abuf = sdb("abuf", [128, 32, 512], BF16)
    bbuf = sdb("bbuf", [128, 32, 512], BF16)
    ytmp = [sdb(f"ytmp{i}", [128, 512], F32) for i in range(2)]

    x2_lastuse = {}
    xn2_lastuse = {}
    ev_done = {}
    mul_done = {}
    w_lastuse = {}
    y_add_done = {}

    if wmirror:
        wait("sync", "gm0", phasew_gm)

    def emit_ffn_body(b):
        xb = xbuf[(b + 1) % 2]
        xn = xn2_fm[b % 2]
        wait("tensor", "dv", xn_rdy_d[b])
        for oc in range(32):
            widx = b * 32 + oc
            wbuf = widx % 3
            wait("sync", "pe", semA_last[wbuf])
            dma("sync", wb12[wbuf][:, 0, :, :], wsrc["w1"][oc], KSEM[wbuf])
            dma("sync", wb12[wbuf][:, 1, :, :], wsrc["w2"][oc], KSEM[wbuf])
            wdma = C[KSEM[wbuf]]
            bA = pf[2 * (oc % 2)]
            bB = pf[2 * (oc % 2) + 1]
            wait("tensor", KSEM[wbuf], wdma)
            if oc >= 2:
                wait("tensor", "ac", ev_done.get(("w12", widx - 2), 0))
            elif b >= 1:
                wait("tensor", "ac", ev_done[("w3", b * 8 - 1)])
                wait("tensor", "dv", y_add_done[("add", b * 8 - 1)])
            for ic in range(8):
                on("tensor", "matmul", bA[:], wb12[wbuf][:, 0, ic, :], xn[:, ic, :], start=(ic == 0), stop=(ic == 7))
            for ic in range(8):
                on("tensor", "matmul", bB[:], wb12[wbuf][:, 1, ic, :], xn[:, ic, :], start=(ic == 0), stop=(ic == 7),
                   inc=("pe" if ic == 7 else None), n=(1 if ic == 7 else None))
            w_lastuse[("w12", widx)] = C["pe"]
            semA_last[wbuf] = C["pe"]
            snap_mm = C["pe"]
            wait("scalar", "pe", snap_mm)
            if oc >= 2 or b >= 1:
                wait("scalar", "dv", mul_done.get(widx - 2, 0))
            on("scalar", "activation", su[oc % 2][:], bA[:], AF.Silu, scale=scc["s1"][:, oc:oc + 1])
            on("scalar", "activation", sv2[oc % 2][:], bB[:], AF.Identity, scale=scc["s2"][:, oc:oc + 1], inc="ac", n=1)
            ev_done[("w12", widx)] = C["ac"]
            wait("vector", "ac", C["ac"])
            on("vector", "tensor_mul", abuf[:, oc, :], su[oc % 2][:], sv2[oc % 2][:], inc="dv", n=1)
            mul_done[widx] = C["dv"]
            if oc == 3:
                flush_sqrt()
        wait("tensor", "dv", mul_done[b * 32 + 31])
        for oc in range(32):
            widx = b * 32 + oc
            wbuf = widx % 3
            wait("sync", "pe", semB_last[wbuf])
            dma("sync", wbh[wbuf][:], wsrc["h1"][oc], VSEM[wbuf])
            wdma = C[VSEM[wbuf]]
            bank = pf[oc % 2]
            wait("tensor", VSEM[wbuf], wdma)
            if oc >= 2 or b >= 1:
                wait("tensor", "ac", ev_done.get(("h1", widx - 2), 0))
            elif b == 0:
                wait("tensor", "ac", ev_done[("w12", 31)])
            for ic in range(32):
                on("tensor", "matmul", bank[:], wbh[wbuf][:, ic, :], abuf[:, ic, :], start=(ic == 0), stop=(ic == 31),
                   inc=("pe" if ic == 31 else None), n=(1 if ic == 31 else None))
            w_lastuse[("h1", widx)] = C["pe"]
            semB_last[wbuf] = C["pe"]
            wait("scalar", "pe", C["pe"])
            on("scalar", "activation", bbuf[:, oc, :], bank[:], AF.Gelu, scale=scc["sh1"][:, oc:oc + 1], inc="ac", n=1)
            ev_done[("h1", widx)] = C["ac"]
        wait("tensor", "ac", ev_done[("h1", b * 32 + 31)])
        for oc in range(32):
            widx = b * 32 + oc
            wbuf = widx % 3
            wait("sync", "pe", semB_last[wbuf])
            dma("sync", wbh[wbuf][:], wsrc["h2"][oc], VSEM[wbuf])
            wdma = C[VSEM[wbuf]]
            bank = pf[oc % 2]
            wait("tensor", VSEM[wbuf], wdma)
            if oc >= 2 or b >= 1:
                wait("tensor", "ac", ev_done.get(("h2", widx - 2), 0))
            for ic in range(32):
                on("tensor", "matmul", bank[:], wbh[wbuf][:, ic, :], bbuf[:, ic, :], start=(ic == 0), stop=(ic == 31),
                   inc=("pe" if ic == 31 else None), n=(1 if ic == 31 else None))
            w_lastuse[("h2", widx)] = C["pe"]
            semB_last[wbuf] = C["pe"]
            wait("scalar", "pe", C["pe"])
            on("scalar", "activation", abuf[:, oc, :], bank[:], AF.Gelu, scale=scc["sh2"][:, oc:oc + 1], inc="ac", n=1)
            ev_done[("h2", widx)] = C["ac"]
        wait("tensor", "ac", ev_done[("h2", b * 32 + 31)])
        for oc in range(8):
            widx = b * 8 + oc
            wbuf = (b * 8 + oc) % 3
            wait("sync", "pe", semB_last[wbuf])
            dma("sync", wbh[wbuf][:], wsrc["w3"][oc], VSEM[wbuf])
            wdma = C[VSEM[wbuf]]
            bank = pf[oc % 2]
            wait("tensor", VSEM[wbuf], wdma)
            if oc >= 2 or b >= 1:
                wait("tensor", "ac", ev_done.get(("w3", widx - 2), 0))
            for ic in range(32):
                on("tensor", "matmul", bank[:], wbh[wbuf][:, ic, :], abuf[:, ic, :], start=(ic == 0), stop=(ic == 31),
                   inc=("pe" if ic == 31 else None), n=(1 if ic == 31 else None))
            w_lastuse[("w3", widx)] = C["pe"]
            semB_last[wbuf] = C["pe"]
            wait("scalar", "pe", C["pe"])
            if oc >= 2 or b >= 1:
                wait("scalar", "pe", y_add_done.get(("tr", widx - 2), 0))
            on("scalar", "activation", ytmp[oc % 2][:], bank[:], AF.Identity, scale=scc["s3"][:, oc:oc + 1], inc="ac", n=1)
            ev_done[("w3", widx)] = C["ac"]

            def y_chain(ocn, widxn):
                yb = pf[4 + ocn % 2]
                wait("tensor", "ac", ev_done[("w3", widxn)])
                if ocn >= 2 or b >= 1:
                    wait("tensor", "dv", y_add_done.get(("add", widxn - 2), 0))
                for t in range(4):
                    on("tensor", "transpose", yb[:, t * 128:(t + 1) * 128], ytmp[ocn % 2][:, t * 128:(t + 1) * 128], identf[:],
                       inc=("pe" if t == 3 else None), n=(1 if t == 3 else None))
                y_add_done[("tr", widxn)] = C["pe"]
                wait("vector", "pe", C["pe"])
                for t in range(4):
                    on("vector", "tensor_tensor", xb[:, t, ocn * 128:(ocn + 1) * 128], yb[:, t * 128:(t + 1) * 128],
                       xb[:, t, ocn * 128:(ocn + 1) * 128], ALU.add,
                       inc=("dv" if t == 3 else None), n=(1 if t == 3 else None))
                y_add_done[("add", widxn)] = C["dv"]
            if oc >= 1:
                y_chain(oc - 1, widx - 1)
            if oc == 7:
                y_chain(7, widx)
        xn2_lastuse[b] = w_lastuse[("w12", b * 32 + 31)]
        x2_lastuse[b] = C["dv"]
        wait("gpsimd", "dv", y_add_done[("add", b * 8 + 7)])
        for t in range(4):
            dma("gpsimd", out_d[b * 512 + t * 128: b * 512 + (t + 1) * 128, :], xb[:, t, :], f"g{6 + b % 2}")
        out_dma[b] = C[f"g{6 + b % 2}"]

    for i in range(NBLK_D + 1):
        if 0 < i < NBLK_D:
            emit_d_load_stats(i)
        if i >= 1:
            emit_ffn_body(i - 1)
        if i < NBLK_D:
            xn_rdy_d[i], _apl = emit_ln_finish(xbuf[(i + 1) % 2], xn2_tm[i % 2], xn2_fm[i % 2], i % 2)

    wait("sync", "g6", C["g6"])
    wait("sync", "g7", C["g7"])
    wait("gpsimd", "g6", C["g6"])
    wait("gpsimd", "g7", C["g7"])

    with nc.Block() as blk:
        @blk.sync
        def _(g):
            for f in EV["sync"]:
                f(g)

        @blk.gpsimd
        def _(g):
            for f in EV["gpsimd"]:
                f(g)

        @blk.tensor
        def _(g):
            for f in EV["tensor"]:
                f(g)

        @blk.vector
        def _(g):
            for f in EV["vector"]:
                f(g)

        @blk.scalar
        def _(g):
            for f in EV["scalar"]:
                f(g)

    es_d.close()
    es.close()
    return nc



def _tile_w(w):
    o, i = w.shape
    wt = np.ascontiguousarray(w.T)
    t = wt.reshape(i // 128, 128, o // 128, 128)
    return np.ascontiguousarray(t.transpose(2, 1, 0, 3))


def _maybe_bf16(a):
    ab = a.astype(ml_dtypes.bfloat16)
    if (ab.astype(np.float32) == a).all():
        return ab, True
    return a, False


def _host_prep(inp):
    x = np.asarray(inp["L0_attn_hidden_states"], np.float32)
    wqk = np.asarray(inp["L0_attn_Wqk"], np.float32)
    bqk = np.asarray(inp["L0_attn_bqk"], np.float32)
    wv = np.asarray(inp["L0_attn_Wv"], np.float32)
    bv = np.asarray(inp["L0_attn_bv"], np.float32)
    ws = {}
    flags = {}
    for nm, key in (("w1", "L0_ffn_w1_w"), ("w2", "L0_ffn_w2_w"), ("h1", "L0_ffn_h1_w"),
                    ("h2", "L0_ffn_h2_w"), ("w3", "L0_ffn_w3_w")):
        t = _tile_w(np.asarray(inp[key], np.float32))
        t2, ok = _maybe_bf16(t)
        ws[nm] = t2
        flags[nm] = ok
    scales = {"s1": inp["L0_ffn_w1_s"], "s2": inp["L0_ffn_w2_s"], "sh1": inp["L0_ffn_h1_s"],
              "sh2": inp["L0_ffn_h2_s"], "s3": inp["L0_ffn_w3_s"]}

    ident = np.eye(128, dtype=ml_dtypes.bfloat16)
    identf = np.eye(128, dtype=np.float32)
    d = np.arange(128)
    hmat = (d[None, :] >= d[:, None]).astype(ml_dtypes.bfloat16)

    def build_g(cp):
        gmk = np.zeros((8, 128, 512), np.float32)
        for r in range(8):
            for t in range(4):
                dstate = 2 * t + cp
                if r == dstate:
                    for qr in range(127):
                        gmk[r, qr + 1, 128 * t + qr] = MASKB
                elif r > dstate:
                    gmk[r, 0, 128 * t:128 * t + 128] = MASKB
        return gmk.astype(ml_dtypes.bfloat16)

    gms = [build_g(0), build_g(1)]
    in_maps = []
    for core in range(8):
        b, cp = core >> 1, core & 1
        xb = x[b]
        xq = np.ascontiguousarray(xb.reshape(32, 128, D)[cp::2].reshape(2048, D))
        m = {
            "x_kv": xb, "x_q": xq,
            "wqk": wqk, "bqk": np.ascontiguousarray(bqk.reshape(16, 128).T),
            "wv": wv, "bv": bv.reshape(1, D),
            "ident": ident, "identf": identf, "hm": hmat, "gm": gms[cp],
        }
        for nm in ws:
            m["w_" + nm] = ws[nm]
        for nm in ("s1", "s2", "sh1", "sh2", "s3"):
            sc = np.asarray(scales[nm], np.float32)
            m[nm] = np.ascontiguousarray(sc.reshape(-1, 128).T)
        in_maps.append(m)
    return flags, in_maps


def kernel(**inp):
    flags, in_maps = _host_prep(inp)
    nc = _build(flags)
    res = run_bass_kernel_spmd(nc, in_maps, core_ids=list(range(8)))
    out = np.empty((B, S, D), np.float32)
    for core in range(8):
        b, cp = core >> 1, core & 1
        oc = res.results[core]["out"].reshape(16, 128, D)
        out.reshape(B, 32, 128, D)[b, cp::2] = oc
    return out


# version 23
# speedup vs baseline: 1.0084x; 1.0084x over previous
import numpy as np
import ml_dtypes
import concourse.bass as bass
import concourse.mybir as mybir
from concourse.bass_utils import run_bass_kernel_spmd
from contextlib import ExitStack

F32 = mybir.dt.float32
BF16 = mybir.dt.bfloat16
AF = mybir.ActivationFunctionType
ALU = mybir.AluOpType
AX = mybir.AxisListType

B, S, D = 4, 4096, 1024
NH, HD = 16, 64
NT = S // 128
OWN = NT // 2
NBLK_A = S // 512
NBLK_B = 2048 // 512
NGRP = 4
NBLK_D = 2048 // 512
DH = 4096
EPS = 1e-5
MASKB = -32768.0

ENGS = ("sync", "gpsimd", "tensor", "vector", "scalar")


def _build(ffn_bf16_flags):
    ffn_all_bf16 = all(ffn_bf16_flags.values())
    nc = bass.Bass("TRN2", target_bir_lowering=False)

    x_kv = nc.dram_tensor("x_kv", [S, D], F32, kind="ExternalInput").ap()
    x_q = nc.dram_tensor("x_q", [2048, D], F32, kind="ExternalInput").ap()
    wqk_d = nc.dram_tensor("wqk", [D, 2 * D], F32, kind="ExternalInput").ap()
    bqk_d = nc.dram_tensor("bqk", [128, 16], F32, kind="ExternalInput").ap()
    wv_d = nc.dram_tensor("wv", [D, D], F32, kind="ExternalInput").ap()
    bv_d = nc.dram_tensor("bv", [1, D], F32, kind="ExternalInput").ap()
    ident_d = nc.dram_tensor("ident", [128, 128], BF16, kind="ExternalInput").ap()
    identf_d = nc.dram_tensor("identf", [128, 128], F32, kind="ExternalInput").ap()
    hm_d = nc.dram_tensor("hm", [128, 128], BF16, kind="ExternalInput").ap()
    gm_d = nc.dram_tensor("gm", [8, 128, 512], BF16, kind="ExternalInput").ap()
    wspec = {"w1": 32, "w2": 32, "h1": (32, 32), "h2": (32, 32), "w3": (8, 32)}
    wdim = {"w1": (32, 8), "w2": (32, 8), "h1": (32, 32), "h2": (32, 32), "w3": (8, 32)}
    wdram = {}
    wmirror = {}
    for nm, (no, ni) in wdim.items():
        dt = BF16 if ffn_bf16_flags[nm] else F32
        wdram[nm] = nc.dram_tensor("w_" + nm, [no, 128, ni, 128], dt, kind="ExternalInput").ap()
        if not ffn_bf16_flags[nm]:
            wmirror[nm] = nc.dram_tensor("wm_" + nm, [no, 128, ni, 128], BF16, kind="Internal").ap()
    sc_d = {}
    for nm, w in (("s1", 32), ("s2", 32), ("sh1", 32), ("sh2", 32), ("s3", 8)):
        sc_d[nm] = nc.dram_tensor(nm, [128, w], F32, kind="ExternalInput").ap()
    k_sp = nc.dram_tensor("k_sp", [D, S], BF16, kind="Internal").ap()
    v_sp = nc.dram_tensor("v_sp", [S, 1040], BF16, kind="Internal").ap()
    x2_sp = nc.dram_tensor("x2_sp", [2048, D], F32, kind="Internal").ap()
    out_d = nc.dram_tensor("out", [2048, D], F32, kind="ExternalOutput").ap()

    es = ExitStack()
    sb = lambda name, shape, dt: es.enter_context(nc.sbuf_tensor(name, shape, dt))
    ident = sb("identb", [128, 128], BF16)
    identf = sb("identfb", [128, 128], F32)
    hm = sb("hmb", [128, 128], BF16)
    gm = sb("gmb", [128, 8, 512], BF16)
    ones_col = sb("ones_col", [1, 128], BF16)
    e0m = sb("e0m", [128, 128], BF16)
    ohm = sb("ohm", [128, 128], BF16)
    sel0 = sb("sel0", [2, 128], BF16)
    sel1 = sb("sel1", [2, 128], BF16)
    bv_pad = sb("bv_pad", [128, 1024], BF16)
    bqk_c = sb("bqk_c", [128, 16], F32)
    scc = {nm: sb("c_" + nm, [128, w], F32) for nm, w in
           (("s1", 32), ("s2", 32), ("sh1", 32), ("sh2", 32), ("s3", 8))}
    kn_part = sb("kn_part", [2, 64], F32)
    qn_part = sb("qn_part", [2, 32], F32)
    kmax = sb("kmax", [2, 8], F32)
    m2 = sb("m2", [2, 32], F32)
    msm = sb("msm", [2, 32], BF16)
    mb = sb("mb", [128, 64], F32)
    xbuf = [sb(f"xbuf{i}", [128, 4, 1024], F32) for i in range(3)]
    stats = [sb(f"stats{i}", [128, 48], F32) for i in range(2)]
    mv = [sb(f"mv{i}", [128, 8], F32) for i in range(2)]
    tmp4 = [sb(f"tmp4_{i}", [128, 4], F32) for i in range(2)]
    rec4 = [sb(f"rec4_{i}", [128, 4], F32) for i in range(2)]
    rstd4 = [sb(f"rstd4_{i}", [128, 4], F32) for i in range(2)]
    nb4 = [sb(f"nb4_{i}", [128, 4], F32) for i in range(2)]
    mbp = sb("mbp", [128, 32], F32)

    ao = [sb(f"ao{i}", [65, 512], F32) for i in range(2)]
    zc = sb("zc", [128, 4], F32)
    zr = sb("zr", [128, 4], F32)

    es_ps = ExitStack()
    pf = [es_ps.enter_context(nc.psum_tensor(f"pf{i}", [128, 512], F32)) for i in range(6)]
    pb = [es_ps.enter_context(nc.psum_tensor(f"pb{i}", [128, 1024], BF16)) for i in range(2)]

    semnames = (["dc", "dg", "sg", "sh", "sw"] + [f"d{i}" for i in range(8)] + [f"g{i}" for i in range(8)]
                + ["gm0", "pe", "dv", "ac", "ph"])
    sem = {n: es.enter_context(nc.semaphore("s_" + n)) for n in semnames}
    C = {n: 0 for n in sem}

    EV = {e: [] for e in ENGS}

    def on(e, meth, *args, inc=None, n=None, **kw):
        def f(g, meth=meth, args=args, kw=kw, inc=inc, n=n):
            ins = getattr(g, meth)(*args, **kw)
            if inc is not None:
                ins.then_inc(sem[inc], n)
        EV[e].append(f)
        if inc is not None:
            C[inc] += n

    def wait(e, s, v):
        if v > 0:
            EV[e].append(lambda g, s=s, v=v: g.wait_ge(sem[s], v))

    def dma(e, out, in_, inc):
        on(e, "dma_start", out=out, in_=in_, inc=inc, n=16)

    def barrier():
        for e in ENGS:
            on(e, "nop", inc="ph", n=1)
        tgt = C["ph"]
        for e in ENGS:
            wait(e, "ph", tgt)

    if not ffn_all_bf16:
        pass
    else:
        dma("sync", xbuf[0][:], x_kv[0:512, :].rearrange("(t p) d -> p t d", p=128), "d0")
        dma("sync", xbuf[1][:], x_kv[512:1024, :].rearrange("(t p) d -> p t d", p=128), "d1")
    dma("sync", ident[:], ident_d, "dc")
    dma("sync", identf[:], identf_d, "dc")
    dma("sync", hm[:], hm_d, "dc")
    dma("sync", bqk_c[:], bqk_d, "dc")
    for nm in scc:
        dma("sync", scc[nm][:], sc_d[nm], "dc")
    es_q = ExitStack()
    q_fm = es_q.enter_context(nc.sbuf_tensor("q_fm", [128, 8, 2048], BF16))
    bv_f = es_q.enter_context(nc.sbuf_tensor("bv_f", [1, D], F32))
    dma("sync", bv_f[:], bv_d, "dc")
    dc_total = C["dc"]
    wait("vector", "dc", dc_total)
    on("vector", "memset", ones_col[:], 1.0)
    on("vector", "memset", e0m[:], 0.0)
    on("vector", "memset", ohm[:], 0.0)
    on("vector", "memset", sel0[:], 0.0)
    on("vector", "memset", sel1[:], 1.0)
    on("vector", "memset", bv_pad[:], 0.0)
    on("vector", "drain")
    on("vector", "memset", e0m[0:1, :], 1.0)
    on("vector", "memset", ohm[0:64, 0:1], 1.0)
    on("vector", "memset", ohm[64:128, 1:2], 1.0)
    on("vector", "memset", sel0[0:1, :], 1.0)
    on("vector", "memset", sel1[0:1, :], 0.0)
    on("vector", "tensor_copy", bv_pad[0:1, :], bv_f[:], inc="dv", n=1)
    es_ab = ExitStack()
    sab = lambda name, shape, dt: es_ab.enter_context(nc.sbuf_tensor(name, shape, dt))
    wqk_bf = sab("wqk_bf", [128, 8, 2048], BF16)
    wv_bf = sab("wv_bf", [128, 8, 1024], BF16)
    xf = xbuf[0][:].rearrange("p a d -> p (a d)")
    xf2 = xbuf[1][:].rearrange("p a d -> p (a d)")
    xf3 = xbuf[2][:].rearrange("p a d -> p (a d)")
    pre_snap = {0: 16, 1: 16}
    dma("sync", gm[:], gm_d.rearrange("r p q -> p r q"), "dg")
    gm_snap = C["dg"]
    stage_jobs = []
    for ic in range(8):
        stage_jobs.append((wqk_d.rearrange("(i p) o -> p i o", p=128)[:, ic:ic + 1, 1024:2048],
                           wqk_bf[:, ic:ic + 1, 1024:2048], 1024, 1024))
    for half in range(4):
        stage_jobs.append((wv_d.rearrange("(i p) o -> p i o", p=128)[:, 2 * half:2 * half + 2, :],
                           wv_bf[:, 2 * half:2 * half + 2, :], 1024, 2048))
    for ic in range(8):
        stage_jobs.append((wqk_d.rearrange("(i p) o -> p i o", p=128)[:, ic:ic + 1, 0:1024],
                           wqk_bf[:, ic:ic + 1, 0:1024], 1024, 1024))
    for j, (src_ap, dst_ap, owid, nel) in enumerate(stage_jobs):
        hv = xf3[:, (j % 2) * 2048:(j % 2) * 2048 + nel].rearrange("p (i o) -> p i o", o=owid)
        ssem = "d2" if j % 2 == 0 else "sh"
        if j >= 2:
            wait("sync", "sg", j - 1)
        dma("sync", hv, src_ap, ssem)
        wait("gpsimd", ssem, C[ssem])
        on("gpsimd", "tensor_copy", dst_ap, hv, inc="sg", n=1)
    stage_ph = C["sg"]
    stage_k, stage_v, stage_q = stage_ph - 12, stage_ph - 8, stage_ph

    if wmirror:
        wait("sync", "sg", C["sg"])
    for nm in wmirror:
        no, ni = wdim[nm]
        for oc in range(no):
            wait("sync", "sw", C["sw"])
            dma("sync", xf3.rearrange("p (i o) -> p i o", o=128)[:, 0:ni, :],
                wdram[nm][oc], "d2")
            wait("vector", "d2", C["d2"])
            wait("vector", "gm0", C["gm0"])
            on("vector", "tensor_copy", xf2[:, 0:ni * 128], xf3[:, 0:ni * 128], inc="sw", n=1)
            wait("gpsimd", "sw", C["sw"])
            dma("gpsimd", wmirror[nm][oc],
                xf2.rearrange("p (i o) -> p i o", o=128)[:, 0:ni, :], "gm0")
    wsrc = {nm: (wmirror[nm] if nm in wmirror else wdram[nm]) for nm in wdim}
    phasew_gm = C["gm0"]

    xn_tm = [sab(f"xn_tm{i}", [128, 4, 1024], BF16) for i in range(2)]
    xn_fm = [sab(f"xn_fm{i}", [128, 8, 512], BF16) for i in range(2)]
    kst = [sab(f"kst{i}", [128, 512], BF16) for i in range(2)]
    ksq = [sab(f"ksq{i}", [128, 512], BF16) for i in range(2)]
    vstage = [sab(f"vstage{i}", [128, 1040], BF16) for i in range(2)]

    for i in range(2):
        on("vector", "memset", vstage[i][:].rearrange("p (h c) -> p h c", c=65)[:, :, 64:65], 1.0)
    on("vector", "nop", inc="dv", n=1)
    wait("tensor", "dc", dc_total)
    wait("scalar", "dc", dc_total)
    if wmirror:
        wait("tensor", "sg", stage_ph)
        barrier()
        wait("sync", "gm0", C["gm0"])
        dma("sync", xbuf[0][:], x_kv[0:512, :].rearrange("(t p) d -> p t d", p=128), "d0")
        pre_snap[0] = C["d0"]
        dma("sync", xbuf[1][:], x_kv[512:1024, :].rearrange("(t p) d -> p t d", p=128), "d1")
        pre_snap[1] = C["d1"]

    ln_calls = [0]
    pb_last = {0: 0, 1: 0}
    pending_sqrt = [None]
    ln_snb = [0]

    def flush_sqrt():
        if pending_sqrt[0] is not None:
            pending_sqrt[0]()
            pending_sqrt[0] = None

    def emit_ln_stats(xb, xsem, xdma_snap, p):
        st, mvp, tm, rc = stats[p], mv[p], tmp4[p], rec4[p]
        wait("vector", xsem, xdma_snap)
        on("vector", "drain")
        for t in range(4):
            on("vector", "bn_stats", st[:, t * 12:t * 12 + 6], xb[:, t, 0:512])
            on("vector", "bn_stats", st[:, t * 12 + 6:t * 12 + 12], xb[:, t, 512:1024])
        on("vector", "drain")
        for t in range(4):
            on("vector", "bn_aggr", mvp[:, 2 * t:2 * t + 2], st[:, t * 12:t * 12 + 12])
        on("vector", "drain")
        on("vector", "tensor_scalar_add", tm[:], mvp[:].rearrange("p (t s) -> p t s", s=2)[:, :, 1], EPS)
        on("vector", "drain")
        on("vector", "reciprocal", rc[:], tm[:], inc="dv", n=1)
        snap_dv = C["dv"]

        def ln_sqrt_part(snap_dv=snap_dv, p=p):
            wait("scalar", "dv", snap_dv)
            on("scalar", "drain")
            on("scalar", "activation", rstd4[p][:], rec4[p][:], AF.Sqrt, inc="ac", n=1)
            on("scalar", "drain")
            snap_ac = C["ac"]
            wait("vector", "ac", snap_ac)
            on("vector", "scalar_tensor_tensor", nb4[p][:], mv[p][:].rearrange("p (t s) -> p t s", s=2)[:, :, 0], -1.0, rstd4[p][:], ALU.mult, ALU.mult, inc="dv", n=1)
            ln_snb[0] = C["dv"]
        pending_sqrt[0] = ln_sqrt_part
        return None

    def emit_ln_finish(xb, xnt, xn, p):
        flush_sqrt()
        wait("scalar", "dv", ln_snb[0])
        for t in range(4):
            on("scalar", "activation", xnt[:, t, :], xb[:, t, :], AF.Identity,
               scale=rstd4[p][:, t:t + 1], bias=nb4[p][:, t:t + 1],
               inc=("ac" if t == 3 else None), n=(1 if t == 3 else None))
        snap_apply = C["ac"]
        wait("tensor", "ac", snap_apply)
        for t in range(4):
            for half in range(2):
                bank = pb[half]
                wait("tensor", "dv", pb_last[half])
                for c4 in range(4):
                    c = half * 4 + c4
                    on("tensor", "transpose", bank[:, c4 * 128:(c4 + 1) * 128],
                       xnt[:, t, c * 128:(c + 1) * 128], ident[:],
                       inc=("pe" if c4 == 3 else None), n=(1 if c4 == 3 else None))
                snap_tr = C["pe"]
                wait("vector", "pe", snap_tr)
                on("vector", "tensor_copy",
                   xn[:, half * 4:half * 4 + 4, t * 128:(t + 1) * 128],
                   bank[:, 0:512].rearrange("p (c q) -> p c q", c=4),
                   inc="dv", n=1)
                pb_last[half] = C["dv"]
        return C["dv"], snap_apply

    xa_lastuse = {}
    xn_ready = {}
    kst_spill = {}
    ksq_use = {}
    vstage_spill = {}
    vstage_ready = {}
    kn_red = {}
    qn_red = {}
    xn_lastuse = {}

    def emit_kv_body(a, xn, xnrdy):
        wait("tensor", "dv", xnrdy)
        if a == 0:
            wait("tensor", "sg", stage_k)
        for kc in range(8):
            bank = pf[kc % 2]
            if kc >= 2:
                wait("tensor", "ac", ksq_use.get(("evac", a, kc - 2), 0))
            for ic in range(8):
                on("tensor", "matmul", bank[:],
                   wqk_bf[:, ic, 1024 + kc * 128:1024 + (kc + 1) * 128],
                   xn[:, ic, :], start=(ic == 0), stop=(ic == 7),
                   inc=("pe" if ic == 7 else None), n=(1 if ic == 7 else None))
            snap_mm = C["pe"]
            wait("scalar", "pe", snap_mm)
            idx = a * 8 + kc
            if idx >= 2:
                wait("scalar", f"g{kc % 2}", kst_spill[idx - 2])
            on("scalar", "activation", kst[kc % 2][:], bank[:], AF.Identity,
               bias=bqk_c[:, 8 + kc:9 + kc], inc="ac", n=1)
            if idx >= 2:
                wait("scalar", "pe", ksq_use.get(("nmm", a, kc - 2), 0))
            on("scalar", "activation", ksq[kc % 2][:], bank[:], AF.Square,
               bias=bqk_c[:, 8 + kc:9 + kc], inc="ac", n=1)
            ksq_use[("evac", a, kc)] = C["ac"]
            snap_evac = C["ac"]
            wait("gpsimd", "ac", snap_evac - 1)
            dma("gpsimd", k_sp[kc * 128:(kc + 1) * 128, a * 512:(a + 1) * 512], kst[kc % 2][:], f"g{kc % 2}")
            kst_spill[idx] = C[f"g{kc % 2}"]

            def norm_mm(kcn, idxn):
                wait("tensor", "ac", ksq_use[("evac", a, kcn)])
                if idxn >= 1:
                    wait("tensor", "dv", kn_red[idxn - 1])
                on("tensor", "matmul", pf[2][:], ohm[:], ksq[kcn % 2][:], start=True, stop=True, inc="pe", n=1)
                ksq_use[("nmm", a, kcn)] = C["pe"]
                wait("vector", "pe", C["pe"])
                on("vector", "tensor_reduce", kn_part[0:2, kcn * 8 + a: kcn * 8 + a + 1], pf[2][0:2, :], AX.X, ALU.max, inc="dv", n=1)
                kn_red[idxn] = C["dv"]
            if kc == 3:
                flush_sqrt()
            if kc >= 1:
                norm_mm(kc - 1, idx - 1)
            if kc == 7:
                norm_mm(7, idx)
        if a == 0:
            wait("tensor", "sg", stage_v)
        for t in range(4):
            banks = (pf[4], pf[5]) if t % 2 == 0 else (pf[0], pf[1])
            vidx = a * 4 + t
            if t % 2 == 1:
                wait("tensor", "ac", ksq_use[("evac", a, 7)])
            if vidx >= 2:
                wait("tensor", "dv", vstage_ready[vidx - 2])
            for half in range(2):
                bank = banks[half]
                on("tensor", "matmul", bank[:], e0m[:], bv_pad[:, half * 512:(half + 1) * 512], start=True, stop=False)
                for ic in range(8):
                    on("tensor", "matmul", bank[:],
                       xn[:, ic, t * 128:(t + 1) * 128],
                       wv_bf[:, ic, half * 512:(half + 1) * 512],
                       start=False, stop=(ic == 7),
                       inc=("pe" if ic == 7 else None), n=(1 if ic == 7 else None))
            snap_vmm = C["pe"]
            wait("vector", "pe", snap_vmm)
            if vidx >= 2:
                wait("vector", f"g{2 + t % 2}", vstage_spill[vidx - 2])
            for half in range(2):
                on("vector", "tensor_copy",
                   vstage[t % 2][:].rearrange("p (h c) -> p h c", c=65)[:, half * 8:half * 8 + 8, 0:64],
                   banks[half][:].rearrange("p (h d) -> p h d", d=64),
                   inc=("dv" if half == 1 else None), n=(1 if half == 1 else None))
            vstage_ready[vidx] = C["dv"]
            wait("gpsimd", "dv", vstage_ready[vidx])
            dma("gpsimd", v_sp[a * 512 + t * 128: a * 512 + (t + 1) * 128, :], vstage[t % 2][:], f"g{2 + t % 2}")
            vstage_spill[vidx] = C[f"g{2 + t % 2}"]

    def emit_q_body(b, xn, xnrdy):
        wait("tensor", "dv", xnrdy)
        if b == 0:
            wait("tensor", "sg", stage_q)
        for qc in range(8):
            bank = pf[qc % 2]
            idx = b * 8 + qc
            if qc >= 2:
                wait("tensor", "ac", ksq_use.get(("qevac", b, qc - 2), 0))
            elif b == 0 and qc < 2:
                wait("tensor", "dv", vstage_ready[NBLK_A * 4 - 1])
            for ic in range(8):
                on("tensor", "matmul", bank[:],
                   wqk_bf[:, ic, qc * 128:(qc + 1) * 128],
                   xn[:, ic, :], start=(ic == 0), stop=(ic == 7),
                   inc=("pe" if ic == 7 else None), n=(1 if ic == 7 else None))
            snap_mm = C["pe"]
            wait("scalar", "pe", snap_mm)
            on("scalar", "activation", q_fm[:, qc, b * 512:(b + 1) * 512], bank[:],
               AF.Identity, bias=bqk_c[:, qc:qc + 1], inc="ac", n=1)
            if idx >= 2:
                wait("scalar", "pe", ksq_use.get(("qnmm", b, qc - 2), 0))
            on("scalar", "activation", ksq[qc % 2][:], bank[:], AF.Square,
               bias=bqk_c[:, qc:qc + 1], inc="ac", n=1)
            ksq_use[("qevac", b, qc)] = C["ac"]

            def qnorm_mm(qcn, idxn):
                wait("tensor", "ac", ksq_use[("qevac", b, qcn)])
                if idxn >= 1:
                    wait("tensor", "dv", qn_red[idxn - 1])
                on("tensor", "matmul", pf[2][:], ohm[:], ksq[qcn % 2][:], start=True, stop=True, inc="pe", n=1)
                ksq_use[("qnmm", b, qcn)] = C["pe"]
                wait("vector", "pe", C["pe"])
                on("vector", "tensor_reduce", qn_part[0:2, qcn * 4 + b: qcn * 4 + b + 1], pf[2][0:2, :], AX.X, ALU.max, inc="dv", n=1)
                qn_red[idxn] = C["dv"]
            if qc == 3:
                flush_sqrt()
            if qc >= 1:
                qnorm_mm(qc - 1, idx - 1)
            if qc == 7:
                qnorm_mm(7, idx)

    NALL = NBLK_A + NBLK_B
    binfo = [("A", i) for i in range(NBLK_A)] + [("B", i) for i in range(NBLK_B)]
    snb = {}
    for i in range(NALL + 1):
        if i < NALL:
            ph, idx = binfo[i]
            xb = xbuf[i % 2]
            if i >= 2:
                wait("sync", "ac", xa_lastuse[i - 2])
                src_ap = (x_kv[idx * 512:(idx + 1) * 512, :] if ph == "A"
                          else x_q[idx * 512:(idx + 1) * 512, :])
                dma("sync", xb[:], src_ap.rearrange("(t p) d -> p t d", p=128), f"d{i % 2}")
                snap_x = C[f"d{i % 2}"]
            else:
                snap_x = pre_snap[i]
            snb[i] = emit_ln_stats(xb, f"d{i % 2}", snap_x, i % 2)
        if i >= 1:
            ph1, idx1 = binfo[i - 1]
            if ph1 == "A":
                emit_kv_body(idx1, xn_fm[(i - 1) % 2], xn_ready[i - 1])
            else:
                emit_q_body(idx1, xn_fm[(i - 1) % 2], xn_ready[i - 1])
        if i < NALL:
            xn_ready[i], xa_lastuse[i] = emit_ln_finish(xbuf[i % 2], xn_tm[i % 2], xn_fm[i % 2], i % 2)
    barrier()
    es_ab.close()
    es_ps.close()

    es_c = ExitStack()
    scb = lambda name, shape, dt: es_c.enter_context(nc.sbuf_tensor(name, shape, dt))
    psc = [es_c.enter_context(nc.psum_tensor(f"psc{i}", [128, 1024], F32)) for i in range(2)]
    pav = [es_c.enter_context(nc.psum_tensor(f"pav{i}", [128, 512], F32)) for i in range(4)]
    vb = [scb(f"vb{i}", [128, 32, 130], BF16) for i in range(3)]
    kb = [scb(f"kb{i}", [128, 4096], BF16) for i in range(3)]
    ptb = scb("ptb", [128, 16, 1024], BF16)

    for gs in ("g0", "g1", "g2", "g3"):
        wait("scalar", gs, C[gs])
    wait("tensor", "dg", gm_snap)
    on("vector", "tensor_reduce", kmax[:], kn_part[:].rearrange("p (h a) -> p h a", a=8), AX.X, ALU.max)
    on("vector", "drain")
    for g_ in range(NGRP):
        on("vector", "tensor_mul", m2[:, g_ * 8:(g_ + 1) * 8],
           qn_part[:].rearrange("p (h g) -> p h g", g=4)[:, :, g_], kmax[:])
    on("vector", "drain")
    on("vector", "nop", inc="dv", n=1)
    wait("scalar", "dv", C["dv"])
    on("scalar", "activation", msm[:], m2[:], AF.Sqrt, scale=1.0 / 64.0, inc="ac", n=1)
    wait("tensor", "ac", C["ac"])
    on("tensor", "matmul", psc[0][:, 0:32], sel0[:], msm[:], start=True, stop=True)
    on("tensor", "matmul", psc[0][:, 32:64], sel1[:], msm[:], start=True, stop=True, inc="pe", n=1)
    wait("scalar", "pe", C["pe"])
    on("scalar", "activation", mb[:], psc[0][:, 0:64], AF.Identity, scale=-1.0, inc="ac", n=1)
    on("scalar", "drain")
    snapm = C["ac"]
    wait("vector", "ac", snapm)
    on("vector", "tensor_tensor", mbp[:], mb[:, 0:32], mb[:, 32:64], ALU.min, inc="dv", n=1)
    wait("scalar", "dv", C["dv"])
    mb_ready = C["ac"]

    kb_lastuse = {}
    vb_lastuse = {}
    pending_tail = [None]
    last_ao = [0]
    kdma_snap = {}
    vdma_snap = {}
    av_snap = {}
    KSEM = ["d4", "d5", "d2"]
    VSEM = ["d6", "d7", "d3"]
    semA_last = {0: 0, 1: 0, 2: 0}
    semB_last = {0: 0, 1: 0, 2: 0}

    def emit_kv_load(n):
        if n >= NGRP * 8:
            return
        gl, hl = n // 8, n % 8
        El = 8 * gl + 8
        KEl = El * 128
        wait("sync", "pe", semA_last[n % 3])
        wait("sync", f"g{hl % 2}", kst_spill[(2 * gl + 1) * 8 + hl])
        dma("sync", kb[n % 3][:, 0:KEl], k_sp[hl * 128:(hl + 1) * 128, 0:KEl], KSEM[n % 3])
        kdma_snap[n] = C[KSEM[n % 3]]
        wait("sync", "pe", semB_last[n % 3])
        wait("sync", "g2", vstage_spill[(2 * gl + 1) * 4 + 2])
        wait("sync", "g3", vstage_spill[(2 * gl + 1) * 4 + 3])
        dma("sync", vb[n % 3][:, 0:El, :], v_sp[0:KEl, 130 * hl:130 * (hl + 1)].rearrange("(e p) c -> p e c", p=128), VSEM[n % 3])
        vdma_snap[n] = C[VSEM[n % 3]]
    exp_done = {}
    av_done = {}
    stt_done = {}
    x2b_spill = {}

    emit_kv_load(0)
    emit_kv_load(1)
    snb_d = {}
    xn_rdy_d = {}
    out_dma = {}

    def emit_d_load_stats(b):
        xb = xbuf[(b + 1) % 2]
        if b == 0:
            wait("sync", "g4", 32)
        if b == 1:
            wait("sync", "dv", stt_done[(3, 7, 1)])
        wait("sync", f"g{4 + b % 2}", 16 * (b // 2 + 1))
        if b >= 2:
            wait("sync", f"g{6 + b % 2}", out_dma[b - 2])
        dma("sync", xb[:], x2_sp[b * 512:(b + 1) * 512, :].rearrange("(t p) d -> p t d", p=128), f"d{(b + 1) % 2}")
        snap_x = C[f"d{(b + 1) % 2}"]
        snb_d[b] = emit_ln_stats(xb, f"d{(b + 1) % 2}", snap_x, b % 2)

    for g in range(NGRP):
        if g == 3:
            emit_d_load_stats(0)
        E = 8 * g + 8
        KE = E * 128
        xq = xbuf[0]
        if g >= 1:
            wait("sync", "dv", stt_done[(g - 1, 7, 1)])
        dma("sync", xq[:], x_q[g * 512:(g + 1) * 512, :].rearrange("(t p) d -> p t d", p=128), "d0")
        xq_snap = C["d0"]
        x2b = xbuf[1 + g % 2]
        if g >= 2:
            wait("vector", f"g{4 + g % 2}", x2b_spill[g - 2])
        for hp in range(8):
            h0, h1 = 2 * hp, 2 * hp + 1
            if (g, hp) != (0, 0):
                pass
            n_lin = g * 8 + hp
            emit_kv_load(n_lin + 2)
            kdma = kdma_snap[n_lin]
            vdma = vdma_snap[n_lin]
            wait("tensor", KSEM[(g * 8 + hp) % 3], kdma)
            wait("tensor", "ac", mb_ready)
            n_lin0 = g * 8 + hp
            if n_lin0 >= 2:
                pp2 = ((n_lin0 - 2) // 8, (n_lin0 - 2) % 8)
                wait("scalar", "pe", av_done.get(pp2, 0))
            prevpair = (g, hp - 1) if hp >= 1 else (g - 1, 7)
            wait("tensor", VSEM[(g * 8 + hp) % 3], vdma)

            def av_mms(c, stop):
                on("tensor", "matmul", pav[0][0:65, :], vb[(g * 8 + hp) % 3][:, c, 0:65], ptb[:, (hp % 2) * 8 + c % 8, 0:512], start=(c == 0), stop=stop)
                on("tensor", "matmul", pav[1][0:65, :], vb[(g * 8 + hp) % 3][:, c, 65:130], ptb[:, (hp % 2) * 8 + c % 8, 512:1024], start=(c == 0), stop=stop,
                   inc="pe", n=1)
                av_snap[(g, hp, c)] = C["pe"]

            for c in range(E):
                bank = psc[c % 2]
                bA = bank[:, 0:512]
                bB = bank[:, 512:1024]
                if c >= 2:
                    wait("tensor", "ac", exp_done[(g, hp, c - 2)])
                elif g > 0 or hp > 0:
                    wait("tensor", "ac", exp_done[(prevpair[0], prevpair[1], 8 * prevpair[0] + 8 - 1 - (1 - c))])
                masked = c >= 8 * g
                on("tensor", "matmul", bA, kb[(g * 8 + hp) % 3][0:64, c * 128:(c + 1) * 128],
                   q_fm[0:64, hp, g * 512:(g + 1) * 512], start=True, stop=not masked)
                on("tensor", "matmul", bB, kb[(g * 8 + hp) % 3][64:128, c * 128:(c + 1) * 128],
                   q_fm[64:128, hp, g * 512:(g + 1) * 512], start=True, stop=not masked,
                   inc=(None if masked else "pe"), n=(None if masked else 1))
                if masked:
                    rel = c - 8 * g
                    on("tensor", "matmul", bA, hm[:], gm[:, rel, :], start=False, stop=True)
                    on("tensor", "matmul", bB, hm[:], gm[:, rel, :], start=False, stop=True, inc="pe", n=1)
                snap_s = C["pe"]
                wait("scalar", "pe", snap_s)
                if c >= 8:
                    wait("scalar", "pe", av_snap[(g, hp, c - 8)])
                on("scalar", "activation", ptb[:, (hp % 2) * 8 + c % 8, :], bank[:], AF.Exp, scale=0.125,
                   bias=mbp[:, g * 8 + hp:g * 8 + hp + 1], inc="ac", n=1)
                exp_done[(g, hp, c)] = C["ac"]
                if c == 3 and pending_tail[0] is not None:
                    pending_tail[0]()
                    pending_tail[0] = None
                if c >= 2:
                    if c == 2 and (g > 0 or hp > 0):
                        wait("tensor", "dv", last_ao[0])
                    wait("tensor", "ac", exp_done[(g, hp, c - 2)])
                    av_mms(c - 2, False)
            wait("tensor", "ac", exp_done[(g, hp, E - 2)])
            av_mms(E - 2, False)
            wait("tensor", "ac", exp_done[(g, hp, E - 1)])
            av_mms(E - 1, True)
            av_done[(g, hp)] = C["pe"]
            semA_last[(g * 8 + hp) % 3] = C["pe"]
            semB_last[(g * 8 + hp) % 3] = C["pe"]
            wait("vector", "pe", av_done[(g, hp)])
            for pi in range(2):
                on("vector", "tensor_copy", ao[pi][:], pav[pi][0:65, :],
                   inc=("dv" if pi == 1 else None), n=(1 if pi == 1 else None))
            ao_snap = C["dv"]

            def attn_tail(g=g, hp=hp, ao_snap=ao_snap, xq=xq, x2b=x2b, xq_snap=xq_snap):
                for pi in range(2):
                    h = 2 * hp + pi
                    psT = pav[2 + pi]
                    wait("tensor", "dv", ao_snap)
                    if (g, hp) != (0, 0):
                        pv = (g, hp - 1) if hp >= 1 else (g - 1, 7)
                        wait("tensor", "dv", stt_done.get((pv[0], pv[1], pi), 0))
                    for t in range(4):
                        on("tensor", "transpose", psT[:, t * 65:t * 65 + 65], ao[pi][0:65, t * 128:(t + 1) * 128], identf[0:65, 0:65],
                           inc=("pe" if t == 3 else None), n=(1 if t == 3 else None))
                    snap_tr = C["pe"]
                    wait("vector", "pe", snap_tr)
                    wait("vector", "d0", xq_snap)
                    on("vector", "tensor_copy", zc[:], psT[:, 0:260].rearrange("p (t c) -> p t c", c=65)[:, :, 64])
                    on("vector", "drain")
                    on("vector", "reciprocal", zr[:], zc[:])
                    on("vector", "drain")
                    for t in range(4):
                        on("vector", "scalar_tensor_tensor", x2b[:, t, h * 64:(h + 1) * 64],
                           psT[:, t * 65:t * 65 + 64], zr[:, t:t + 1], xq[:, t, h * 64:(h + 1) * 64],
                           ALU.mult, ALU.add, inc=("dv" if t == 3 else None), n=(1 if t == 3 else None))
                    stt_done[(g, hp, pi)] = C["dv"]
            pending_tail[0] = attn_tail
            last_ao[0] = ao_snap
        if pending_tail[0] is not None:
            pending_tail[0]()
            pending_tail[0] = None
        wait("gpsimd", "dv", stt_done[(g, 7, 1)])
        dma("gpsimd", x2_sp[g * 512:(g + 1) * 512, :].rearrange("(t p) d -> p t d", p=128), x2b[:], f"g{4 + g % 2}")
        x2b_spill[g] = C[f"g{4 + g % 2}"]
    barrier()
    es_c.close()
    es_q.close()

    es_d = ExitStack()
    sdb = lambda name, shape, dt: es_d.enter_context(nc.sbuf_tensor(name, shape, dt))
    xn2_tm = [sdb(f"xn2tm{i}", [128, 4, 1024], BF16) for i in range(2)]
    xn2_fm = [sdb(f"xn2fm{i}", [128, 8, 512], BF16) for i in range(2)]
    wb12 = [sdb(f"wb12_{i}", [128, 2, 8, 128], BF16) for i in range(3)]
    wbh = [sdb(f"wbh{i}", [128, 32, 128], BF16) for i in range(3)]
    su = [sdb(f"su{i}", [128, 512], BF16) for i in range(2)]
    sv2 = [sdb(f"sv2_{i}", [128, 512], BF16) for i in range(2)]
    abuf = sdb("abuf", [128, 32, 512], BF16)
    bbuf = sdb("bbuf", [128, 32, 512], BF16)
    ytmp = [sdb(f"ytmp{i}", [128, 512], F32) for i in range(2)]

    x2_lastuse = {}
    xn2_lastuse = {}
    ev_done = {}
    mul_done = {}
    w_lastuse = {}
    y_add_done = {}

    if wmirror:
        wait("sync", "gm0", phasew_gm)

    def emit_ffn_body(b):
        xb = xbuf[(b + 1) % 2]
        xn = xn2_fm[b % 2]
        wait("tensor", "dv", xn_rdy_d[b])
        for oc in range(32):
            widx = b * 32 + oc
            wbuf = widx % 3
            wait("sync", "pe", semA_last[wbuf])
            dma("sync", wb12[wbuf][:, 0, :, :], wsrc["w1"][oc], KSEM[wbuf])
            dma("sync", wb12[wbuf][:, 1, :, :], wsrc["w2"][oc], KSEM[wbuf])
            wdma = C[KSEM[wbuf]]
            bA = pf[2 * (oc % 2)]
            bB = pf[2 * (oc % 2) + 1]
            wait("tensor", KSEM[wbuf], wdma)
            if oc >= 2:
                wait("tensor", "ac", ev_done.get(("w12", widx - 2), 0))
            elif b >= 1:
                wait("tensor", "ac", ev_done[("w3", b * 8 - 1)])
                wait("tensor", "dv", y_add_done[("add", b * 8 - 1)])
            for ic in range(8):
                on("tensor", "matmul", bA[:], wb12[wbuf][:, 0, ic, :], xn[:, ic, :], start=(ic == 0), stop=(ic == 7))
            for ic in range(8):
                on("tensor", "matmul", bB[:], wb12[wbuf][:, 1, ic, :], xn[:, ic, :], start=(ic == 0), stop=(ic == 7),
                   inc=("pe" if ic == 7 else None), n=(1 if ic == 7 else None))
            w_lastuse[("w12", widx)] = C["pe"]
            semA_last[wbuf] = C["pe"]
            snap_mm = C["pe"]
            wait("scalar", "pe", snap_mm)
            if oc >= 2 or b >= 1:
                wait("scalar", "dv", mul_done.get(widx - 2, 0))
            on("scalar", "activation", su[oc % 2][:], bA[:], AF.Silu, scale=scc["s1"][:, oc:oc + 1])
            on("scalar", "activation", sv2[oc % 2][:], bB[:], AF.Identity, scale=scc["s2"][:, oc:oc + 1], inc="ac", n=1)
            ev_done[("w12", widx)] = C["ac"]
            wait("vector", "ac", C["ac"])
            on("vector", "tensor_mul", abuf[:, oc, :], su[oc % 2][:], sv2[oc % 2][:], inc="dv", n=1)
            mul_done[widx] = C["dv"]
            if oc == 3:
                flush_sqrt()
        wait("tensor", "dv", mul_done[b * 32 + 31])
        for oc in range(32):
            widx = b * 32 + oc
            wbuf = widx % 3
            wait("sync", "pe", semB_last[wbuf])
            dma("sync", wbh[wbuf][:], wsrc["h1"][oc], VSEM[wbuf])
            wdma = C[VSEM[wbuf]]
            bank = pf[oc % 2]
            wait("tensor", VSEM[wbuf], wdma)
            if oc >= 2 or b >= 1:
                wait("tensor", "ac", ev_done.get(("h1", widx - 2), 0))
            elif b == 0:
                wait("tensor", "ac", ev_done[("w12", 31)])
            for ic in range(32):
                on("tensor", "matmul", bank[:], wbh[wbuf][:, ic, :], abuf[:, ic, :], start=(ic == 0), stop=(ic == 31),
                   inc=("pe" if ic == 31 else None), n=(1 if ic == 31 else None))
            w_lastuse[("h1", widx)] = C["pe"]
            semB_last[wbuf] = C["pe"]
            wait("scalar", "pe", C["pe"])
            on("scalar", "activation", bbuf[:, oc, :], bank[:], AF.Gelu, scale=scc["sh1"][:, oc:oc + 1], inc="ac", n=1)
            ev_done[("h1", widx)] = C["ac"]
        wait("tensor", "ac", ev_done[("h1", b * 32 + 31)])
        for oc in range(32):
            widx = b * 32 + oc
            wbuf = widx % 3
            wait("sync", "pe", semB_last[wbuf])
            dma("sync", wbh[wbuf][:], wsrc["h2"][oc], VSEM[wbuf])
            wdma = C[VSEM[wbuf]]
            bank = pf[oc % 2]
            wait("tensor", VSEM[wbuf], wdma)
            if oc >= 2 or b >= 1:
                wait("tensor", "ac", ev_done.get(("h2", widx - 2), 0))
            for ic in range(32):
                on("tensor", "matmul", bank[:], wbh[wbuf][:, ic, :], bbuf[:, ic, :], start=(ic == 0), stop=(ic == 31),
                   inc=("pe" if ic == 31 else None), n=(1 if ic == 31 else None))
            w_lastuse[("h2", widx)] = C["pe"]
            semB_last[wbuf] = C["pe"]
            wait("scalar", "pe", C["pe"])
            on("scalar", "activation", abuf[:, oc, :], bank[:], AF.Gelu, scale=scc["sh2"][:, oc:oc + 1], inc="ac", n=1)
            ev_done[("h2", widx)] = C["ac"]
        wait("tensor", "ac", ev_done[("h2", b * 32 + 31)])
        for oc in range(8):
            widx = b * 8 + oc
            wbuf = (b * 8 + oc) % 3
            wait("sync", "pe", semB_last[wbuf])
            dma("sync", wbh[wbuf][:], wsrc["w3"][oc], VSEM[wbuf])
            wdma = C[VSEM[wbuf]]
            bank = pf[oc % 2]
            wait("tensor", VSEM[wbuf], wdma)
            if oc >= 2 or b >= 1:
                wait("tensor", "ac", ev_done.get(("w3", widx - 2), 0))
            for ic in range(32):
                on("tensor", "matmul", bank[:], wbh[wbuf][:, ic, :], abuf[:, ic, :], start=(ic == 0), stop=(ic == 31),
                   inc=("pe" if ic == 31 else None), n=(1 if ic == 31 else None))
            w_lastuse[("w3", widx)] = C["pe"]
            semB_last[wbuf] = C["pe"]
            wait("scalar", "pe", C["pe"])
            if oc >= 2 or b >= 1:
                wait("scalar", "pe", y_add_done.get(("tr", widx - 2), 0))
            on("scalar", "activation", ytmp[oc % 2][:], bank[:], AF.Identity, scale=scc["s3"][:, oc:oc + 1], inc="ac", n=1)
            ev_done[("w3", widx)] = C["ac"]

            def y_chain(ocn, widxn):
                yb = pf[4 + ocn % 2]
                wait("tensor", "ac", ev_done[("w3", widxn)])
                if ocn >= 2 or b >= 1:
                    wait("tensor", "dv", y_add_done.get(("add", widxn - 2), 0))
                for t in range(4):
                    on("tensor", "transpose", yb[:, t * 128:(t + 1) * 128], ytmp[ocn % 2][:, t * 128:(t + 1) * 128], identf[:],
                       inc=("pe" if t == 3 else None), n=(1 if t == 3 else None))
                y_add_done[("tr", widxn)] = C["pe"]
                wait("vector", "pe", C["pe"])
                for t in range(4):
                    on("vector", "tensor_tensor", xb[:, t, ocn * 128:(ocn + 1) * 128], yb[:, t * 128:(t + 1) * 128],
                       xb[:, t, ocn * 128:(ocn + 1) * 128], ALU.add,
                       inc=("dv" if t == 3 else None), n=(1 if t == 3 else None))
                y_add_done[("add", widxn)] = C["dv"]
            if oc >= 1:
                y_chain(oc - 1, widx - 1)
            if oc == 7:
                y_chain(7, widx)
        xn2_lastuse[b] = w_lastuse[("w12", b * 32 + 31)]
        x2_lastuse[b] = C["dv"]
        wait("gpsimd", "dv", y_add_done[("add", b * 8 + 7)])
        for t in range(4):
            dma("gpsimd", out_d[b * 512 + t * 128: b * 512 + (t + 1) * 128, :], xb[:, t, :], f"g{6 + b % 2}")
        out_dma[b] = C[f"g{6 + b % 2}"]

    for i in range(NBLK_D + 1):
        if 0 < i < NBLK_D:
            emit_d_load_stats(i)
        if i >= 1:
            emit_ffn_body(i - 1)
        if i < NBLK_D:
            xn_rdy_d[i], _apl = emit_ln_finish(xbuf[(i + 1) % 2], xn2_tm[i % 2], xn2_fm[i % 2], i % 2)

    wait("sync", "g6", C["g6"])
    wait("sync", "g7", C["g7"])
    wait("gpsimd", "g6", C["g6"])
    wait("gpsimd", "g7", C["g7"])

    with nc.Block() as blk:
        @blk.sync
        def _(g):
            for f in EV["sync"]:
                f(g)

        @blk.gpsimd
        def _(g):
            for f in EV["gpsimd"]:
                f(g)

        @blk.tensor
        def _(g):
            for f in EV["tensor"]:
                f(g)

        @blk.vector
        def _(g):
            for f in EV["vector"]:
                f(g)

        @blk.scalar
        def _(g):
            for f in EV["scalar"]:
                f(g)

    es_d.close()
    es.close()
    return nc



def _tile_w(w):
    o, i = w.shape
    wt = np.ascontiguousarray(w.T)
    t = wt.reshape(i // 128, 128, o // 128, 128)
    return np.ascontiguousarray(t.transpose(2, 1, 0, 3))


def _maybe_bf16(a):
    ab = a.astype(ml_dtypes.bfloat16)
    if (ab.astype(np.float32) == a).all():
        return ab, True
    return a, False


def _host_prep(inp):
    x = np.asarray(inp["L0_attn_hidden_states"], np.float32)
    wqk = np.asarray(inp["L0_attn_Wqk"], np.float32)
    bqk = np.asarray(inp["L0_attn_bqk"], np.float32)
    wv = np.asarray(inp["L0_attn_Wv"], np.float32)
    bv = np.asarray(inp["L0_attn_bv"], np.float32)
    ws = {}
    flags = {}
    for nm, key in (("w1", "L0_ffn_w1_w"), ("w2", "L0_ffn_w2_w"), ("h1", "L0_ffn_h1_w"),
                    ("h2", "L0_ffn_h2_w"), ("w3", "L0_ffn_w3_w")):
        t = _tile_w(np.asarray(inp[key], np.float32))
        t2, ok = _maybe_bf16(t)
        ws[nm] = t2
        flags[nm] = ok
    scales = {"s1": inp["L0_ffn_w1_s"], "s2": inp["L0_ffn_w2_s"], "sh1": inp["L0_ffn_h1_s"],
              "sh2": inp["L0_ffn_h2_s"], "s3": inp["L0_ffn_w3_s"]}

    ident = np.eye(128, dtype=ml_dtypes.bfloat16)
    identf = np.eye(128, dtype=np.float32)
    d = np.arange(128)
    hmat = (d[None, :] >= d[:, None]).astype(ml_dtypes.bfloat16)

    def build_g(cp):
        gmk = np.zeros((8, 128, 512), np.float32)
        for r in range(8):
            for t in range(4):
                dstate = 2 * t + cp
                if r == dstate:
                    for qr in range(127):
                        gmk[r, qr + 1, 128 * t + qr] = MASKB
                elif r > dstate:
                    gmk[r, 0, 128 * t:128 * t + 128] = MASKB
        return gmk.astype(ml_dtypes.bfloat16)

    gms = [build_g(0), build_g(1)]
    in_maps = []
    for core in range(8):
        b, cp = core >> 1, core & 1
        xb = x[b]
        xq = np.ascontiguousarray(xb.reshape(32, 128, D)[cp::2].reshape(2048, D))
        m = {
            "x_kv": xb, "x_q": xq,
            "wqk": wqk, "bqk": np.ascontiguousarray(bqk.reshape(16, 128).T),
            "wv": wv, "bv": bv.reshape(1, D),
            "ident": ident, "identf": identf, "hm": hmat, "gm": gms[cp],
        }
        for nm in ws:
            m["w_" + nm] = ws[nm]
        for nm in ("s1", "s2", "sh1", "sh2", "s3"):
            sc = np.asarray(scales[nm], np.float32)
            m[nm] = np.ascontiguousarray(sc.reshape(-1, 128).T)
        in_maps.append(m)
    return flags, in_maps


def kernel(**inp):
    flags, in_maps = _host_prep(inp)
    nc = _build(flags)
    res = run_bass_kernel_spmd(nc, in_maps, core_ids=list(range(8)))
    out = np.empty((B, S, D), np.float32)
    for core in range(8):
        b, cp = core >> 1, core & 1
        oc = res.results[core]["out"].reshape(16, 128, D)
        out.reshape(B, 32, 128, D)[b, cp::2] = oc
    return out


# version 25
# speedup vs baseline: 1.0149x; 1.0064x over previous
import numpy as np
import ml_dtypes
import concourse.bass as bass
import concourse.mybir as mybir
from concourse.bass_utils import run_bass_kernel_spmd
from contextlib import ExitStack

F32 = mybir.dt.float32
BF16 = mybir.dt.bfloat16
AF = mybir.ActivationFunctionType
ALU = mybir.AluOpType
AX = mybir.AxisListType

B, S, D = 4, 4096, 1024
NH, HD = 16, 64
NT = S // 128
OWN = NT // 2
NBLK_A = S // 512
NBLK_B = 2048 // 512
NGRP = 4
NBLK_D = 2048 // 512
DH = 4096
EPS = 1e-5
MASKB = -32768.0

ENGS = ("sync", "gpsimd", "tensor", "vector", "scalar")


def _build(ffn_bf16_flags):
    ffn_all_bf16 = all(ffn_bf16_flags.values())
    nc = bass.Bass("TRN2", target_bir_lowering=False)

    x_kv = nc.dram_tensor("x_kv", [S, D], F32, kind="ExternalInput").ap()
    x_q = nc.dram_tensor("x_q", [2048, D], F32, kind="ExternalInput").ap()
    wqk_d = nc.dram_tensor("wqk", [D, 2 * D], F32, kind="ExternalInput").ap()
    bqk_d = nc.dram_tensor("bqk", [128, 16], F32, kind="ExternalInput").ap()
    wv_d = nc.dram_tensor("wv", [D, D], F32, kind="ExternalInput").ap()
    bv_d = nc.dram_tensor("bv", [1, D], F32, kind="ExternalInput").ap()
    ident_d = nc.dram_tensor("ident", [128, 128], BF16, kind="ExternalInput").ap()
    identf_d = nc.dram_tensor("identf", [128, 128], F32, kind="ExternalInput").ap()
    hm_d = nc.dram_tensor("hm", [128, 128], BF16, kind="ExternalInput").ap()
    gm_d = nc.dram_tensor("gm", [8, 128, 512], BF16, kind="ExternalInput").ap()
    wspec = {"w1": 32, "w2": 32, "h1": (32, 32), "h2": (32, 32), "w3": (8, 32)}
    wdim = {"w1": (32, 8), "w2": (32, 8), "h1": (32, 32), "h2": (32, 32), "w3": (8, 32)}
    wdram = {}
    wmirror = {}
    for nm, (no, ni) in wdim.items():
        dt = BF16 if ffn_bf16_flags[nm] else F32
        wdram[nm] = nc.dram_tensor("w_" + nm, [no, 128, ni, 128], dt, kind="ExternalInput").ap()
        if not ffn_bf16_flags[nm]:
            wmirror[nm] = nc.dram_tensor("wm_" + nm, [no, 128, ni, 128], BF16, kind="Internal").ap()
    sc_d = {}
    for nm, w in (("s1", 32), ("s2", 32), ("sh1", 32), ("sh2", 32), ("s3", 8)):
        sc_d[nm] = nc.dram_tensor(nm, [128, w], F32, kind="ExternalInput").ap()
    k_sp = nc.dram_tensor("k_sp", [D, S], BF16, kind="Internal").ap()
    v_sp = nc.dram_tensor("v_sp", [S, 1040], BF16, kind="Internal").ap()
    x2_sp = nc.dram_tensor("x2_sp", [2048, D], F32, kind="Internal").ap()
    out_d = nc.dram_tensor("out", [2048, D], F32, kind="ExternalOutput").ap()

    es = ExitStack()
    sb = lambda name, shape, dt: es.enter_context(nc.sbuf_tensor(name, shape, dt))
    ident = sb("identb", [128, 128], BF16)
    identf = sb("identfb", [128, 128], F32)
    hm = sb("hmb", [128, 128], BF16)
    gm = sb("gmb", [128, 8, 512], BF16)
    ones_col = sb("ones_col", [1, 128], BF16)
    e0m = sb("e0m", [128, 128], BF16)
    ohm = sb("ohm", [128, 128], BF16)
    sel0 = sb("sel0", [2, 128], BF16)
    sel1 = sb("sel1", [2, 128], BF16)
    bv_pad = sb("bv_pad", [128, 1024], BF16)
    bqk_c = sb("bqk_c", [128, 16], F32)
    scc = {nm: sb("c_" + nm, [128, w], F32) for nm, w in
           (("s1", 32), ("s2", 32), ("sh1", 32), ("sh2", 32), ("s3", 8))}
    kn_part = sb("kn_part", [2, 64], F32)
    qn_part = sb("qn_part", [2, 32], F32)
    kmax = sb("kmax", [2, 8], F32)
    m2 = sb("m2", [2, 32], F32)
    msm = sb("msm", [2, 32], BF16)
    mb = sb("mb", [128, 64], F32)
    xbuf = [sb(f"xbuf{i}", [128, 4, 1024], F32) for i in range(3)]
    stats = [sb(f"stats{i}", [128, 48], F32) for i in range(2)]
    mv = [sb(f"mv{i}", [128, 8], F32) for i in range(2)]
    tmp4 = [sb(f"tmp4_{i}", [128, 4], F32) for i in range(2)]
    rec4 = [sb(f"rec4_{i}", [128, 4], F32) for i in range(2)]
    rstd4 = [sb(f"rstd4_{i}", [128, 4], F32) for i in range(2)]
    nb4 = [sb(f"nb4_{i}", [128, 4], F32) for i in range(2)]
    mbp = sb("mbp", [128, 32], F32)

    ao = [sb(f"ao{i}", [65, 512], F32) for i in range(2)]
    zc = sb("zc", [128, 4], F32)
    zr = sb("zr", [128, 4], F32)

    es_ps = ExitStack()
    pf = [es_ps.enter_context(nc.psum_tensor(f"pf{i}", [128, 512], F32)) for i in range(6)]
    pb = [es_ps.enter_context(nc.psum_tensor(f"pb{i}", [128, 1024], BF16)) for i in range(2)]

    semnames = (["dc", "dg", "sg", "sh", "sw"] + [f"d{i}" for i in range(8)] + [f"g{i}" for i in range(8)]
                + ["gm0", "pe", "dv", "ac", "ph"])
    sem = {n: es.enter_context(nc.semaphore("s_" + n)) for n in semnames}
    C = {n: 0 for n in sem}

    EV = {e: [] for e in ENGS}

    def on(e, meth, *args, inc=None, n=None, **kw):
        def f(g, meth=meth, args=args, kw=kw, inc=inc, n=n):
            ins = getattr(g, meth)(*args, **kw)
            if inc is not None:
                ins.then_inc(sem[inc], n)
        EV[e].append(f)
        if inc is not None:
            C[inc] += n

    def wait(e, s, v):
        if v > 0:
            EV[e].append(lambda g, s=s, v=v: g.wait_ge(sem[s], v))

    def dma(e, out, in_, inc):
        on(e, "dma_start", out=out, in_=in_, inc=inc, n=16)

    def barrier():
        for e in ENGS:
            on(e, "nop", inc="ph", n=1)
        tgt = C["ph"]
        for e in ENGS:
            wait(e, "ph", tgt)

    if not ffn_all_bf16:
        pass
    else:
        dma("sync", xbuf[0][:], x_kv[0:512, :].rearrange("(t p) d -> p t d", p=128), "d0")
        dma("sync", xbuf[1][:], x_kv[512:1024, :].rearrange("(t p) d -> p t d", p=128), "d1")
    dma("sync", ident[:], ident_d, "dc")
    dma("sync", identf[:], identf_d, "dc")
    dma("sync", hm[:], hm_d, "dc")
    dma("sync", bqk_c[:], bqk_d, "dc")
    for nm in scc:
        dma("sync", scc[nm][:], sc_d[nm], "dc")
    es_q = ExitStack()
    q_fm = es_q.enter_context(nc.sbuf_tensor("q_fm", [128, 8, 2048], BF16))
    bv_f = es_q.enter_context(nc.sbuf_tensor("bv_f", [1, D], F32))
    dma("sync", bv_f[:], bv_d, "dc")
    dc_total = C["dc"]
    wait("vector", "dc", dc_total)
    on("vector", "memset", ones_col[:], 1.0)
    on("vector", "memset", e0m[:], 0.0)
    on("vector", "memset", ohm[:], 0.0)
    on("vector", "memset", sel0[:], 0.0)
    on("vector", "memset", sel1[:], 1.0)
    on("vector", "memset", bv_pad[:], 0.0)
    on("vector", "drain")
    on("vector", "memset", e0m[0:1, :], 1.0)
    on("vector", "memset", ohm[0:64, 0:1], 1.0)
    on("vector", "memset", ohm[64:128, 1:2], 1.0)
    on("vector", "memset", sel0[0:1, :], 1.0)
    on("vector", "memset", sel1[0:1, :], 0.0)
    on("vector", "tensor_copy", bv_pad[0:1, :], bv_f[:], inc="dv", n=1)
    es_ab = ExitStack()
    sab = lambda name, shape, dt: es_ab.enter_context(nc.sbuf_tensor(name, shape, dt))
    wqk_bf = sab("wqk_bf", [128, 8, 2048], BF16)
    wv_bf = sab("wv_bf", [128, 8, 1024], BF16)
    xf = xbuf[0][:].rearrange("p a d -> p (a d)")
    xf2 = xbuf[1][:].rearrange("p a d -> p (a d)")
    xf3 = xbuf[2][:].rearrange("p a d -> p (a d)")
    pre_snap = {0: 16, 1: 16}
    dma("sync", gm[:], gm_d.rearrange("r p q -> p r q"), "dg")
    gm_snap = C["dg"]
    stage_jobs = []
    for ic in range(8):
        stage_jobs.append((wqk_d.rearrange("(i p) o -> p i o", p=128)[:, ic:ic + 1, 1024:2048],
                           wqk_bf[:, ic:ic + 1, 1024:2048], 1024, 1024))
    for half in range(4):
        stage_jobs.append((wv_d.rearrange("(i p) o -> p i o", p=128)[:, 2 * half:2 * half + 2, :],
                           wv_bf[:, 2 * half:2 * half + 2, :], 1024, 2048))
    for ic in range(8):
        stage_jobs.append((wqk_d.rearrange("(i p) o -> p i o", p=128)[:, ic:ic + 1, 0:1024],
                           wqk_bf[:, ic:ic + 1, 0:1024], 1024, 1024))
    for j, (src_ap, dst_ap, owid, nel) in enumerate(stage_jobs):
        hv = xf3[:, (j % 2) * 2048:(j % 2) * 2048 + nel].rearrange("p (i o) -> p i o", o=owid)
        ssem = "d2" if j % 2 == 0 else "sh"
        if j >= 2:
            wait("sync", "sg", j - 1)
        dma("sync", hv, src_ap, ssem)
        wait("gpsimd", ssem, C[ssem])
        on("gpsimd", "tensor_copy", dst_ap, hv, inc="sg", n=1)
    stage_ph = C["sg"]
    stage_k, stage_v, stage_q = stage_ph - 12, stage_ph - 8, stage_ph

    if wmirror:
        wait("sync", "sg", C["sg"])
    for nm in wmirror:
        no, ni = wdim[nm]
        for oc in range(no):
            wait("sync", "sw", C["sw"])
            dma("sync", xf3.rearrange("p (i o) -> p i o", o=128)[:, 0:ni, :],
                wdram[nm][oc], "d2")
            wait("vector", "d2", C["d2"])
            wait("vector", "gm0", C["gm0"])
            on("vector", "tensor_copy", xf2[:, 0:ni * 128], xf3[:, 0:ni * 128], inc="sw", n=1)
            wait("gpsimd", "sw", C["sw"])
            dma("gpsimd", wmirror[nm][oc],
                xf2.rearrange("p (i o) -> p i o", o=128)[:, 0:ni, :], "gm0")
    wsrc = {nm: (wmirror[nm] if nm in wmirror else wdram[nm]) for nm in wdim}
    phasew_gm = C["gm0"]

    xn_tm = [sab(f"xn_tm{i}", [128, 4, 1024], BF16) for i in range(2)]
    xn_fm = [sab(f"xn_fm{i}", [128, 8, 512], BF16) for i in range(2)]
    kst = [sab(f"kst{i}", [128, 512], BF16) for i in range(2)]
    ksq = [sab(f"ksq{i}", [128, 512], BF16) for i in range(2)]
    vstage = [sab(f"vstage{i}", [128, 1040], BF16) for i in range(2)]

    for i in range(2):
        on("vector", "memset", vstage[i][:].rearrange("p (h c) -> p h c", c=65)[:, :, 64:65], 1.0)
    on("vector", "nop", inc="dv", n=1)
    wait("tensor", "dc", dc_total)
    wait("scalar", "dc", dc_total)
    if wmirror:
        wait("tensor", "sg", stage_ph)
        barrier()
        wait("sync", "gm0", C["gm0"])
        dma("sync", xbuf[0][:], x_kv[0:512, :].rearrange("(t p) d -> p t d", p=128), "d0")
        pre_snap[0] = C["d0"]
        dma("sync", xbuf[1][:], x_kv[512:1024, :].rearrange("(t p) d -> p t d", p=128), "d1")
        pre_snap[1] = C["d1"]

    ln_calls = [0]
    pb_last = {0: 0, 1: 0}
    pending_sqrt = [None]
    ln_snb = [0]

    def flush_sqrt():
        if pending_sqrt[0] is not None:
            pending_sqrt[0]()
            pending_sqrt[0] = None

    def emit_ln_stats(xb, xsem, xdma_snap, p):
        st, mvp, tm, rc = stats[p], mv[p], tmp4[p], rec4[p]
        wait("vector", xsem, xdma_snap)
        on("vector", "drain")
        for t in range(4):
            on("vector", "bn_stats", st[:, t * 12:t * 12 + 6], xb[:, t, 0:512])
            on("vector", "bn_stats", st[:, t * 12 + 6:t * 12 + 12], xb[:, t, 512:1024])
        on("vector", "drain")
        for t in range(4):
            on("vector", "bn_aggr", mvp[:, 2 * t:2 * t + 2], st[:, t * 12:t * 12 + 12])
        on("vector", "drain")
        on("vector", "tensor_scalar_add", tm[:], mvp[:].rearrange("p (t s) -> p t s", s=2)[:, :, 1], EPS)
        on("vector", "drain")
        on("vector", "reciprocal", rc[:], tm[:], inc="dv", n=1)
        snap_dv = C["dv"]

        def ln_sqrt_part(snap_dv=snap_dv, p=p):
            wait("scalar", "dv", snap_dv)
            on("scalar", "drain")
            on("scalar", "activation", rstd4[p][:], rec4[p][:], AF.Sqrt, inc="ac", n=1)
            on("scalar", "drain")
            snap_ac = C["ac"]
            wait("vector", "ac", snap_ac)
            on("vector", "scalar_tensor_tensor", nb4[p][:], mv[p][:].rearrange("p (t s) -> p t s", s=2)[:, :, 0], -1.0, rstd4[p][:], ALU.mult, ALU.mult, inc="dv", n=1)
            ln_snb[0] = C["dv"]
        pending_sqrt[0] = ln_sqrt_part
        return None

    def emit_ln_finish(xb, xnt, xn, p):
        flush_sqrt()
        wait("scalar", "dv", ln_snb[0])
        for t in range(4):
            on("scalar", "activation", xnt[:, t, :], xb[:, t, :], AF.Identity,
               scale=rstd4[p][:, t:t + 1], bias=nb4[p][:, t:t + 1],
               inc=("ac" if t == 3 else None), n=(1 if t == 3 else None))
        snap_apply = C["ac"]
        wait("tensor", "ac", snap_apply)
        for t in range(4):
            for half in range(2):
                bank = pb[half]
                wait("tensor", "dv", pb_last[half])
                for c4 in range(4):
                    c = half * 4 + c4
                    on("tensor", "transpose", bank[:, c4 * 128:(c4 + 1) * 128],
                       xnt[:, t, c * 128:(c + 1) * 128], ident[:],
                       inc=("pe" if c4 == 3 else None), n=(1 if c4 == 3 else None))
                snap_tr = C["pe"]
                wait("vector", "pe", snap_tr)
                on("vector", "tensor_copy",
                   xn[:, half * 4:half * 4 + 4, t * 128:(t + 1) * 128],
                   bank[:, 0:512].rearrange("p (c q) -> p c q", c=4),
                   inc="dv", n=1)
                pb_last[half] = C["dv"]
        return C["dv"], snap_apply

    xa_lastuse = {}
    xn_ready = {}
    kst_spill = {}
    ksq_use = {}
    vstage_spill = {}
    vstage_ready = {}
    kn_red = {}
    qn_red = {}
    xn_lastuse = {}

    def emit_kv_body(a, xn, xnrdy):
        wait("tensor", "dv", xnrdy)
        if a == 0:
            wait("tensor", "sg", stage_k)
        for kc in range(8):
            bank = pf[kc % 2]
            if kc >= 2:
                wait("tensor", "ac", ksq_use.get(("evac", a, kc - 2), 0))
            for ic in range(8):
                on("tensor", "matmul", bank[:],
                   wqk_bf[:, ic, 1024 + kc * 128:1024 + (kc + 1) * 128],
                   xn[:, ic, :], start=(ic == 0), stop=(ic == 7),
                   inc=("pe" if ic == 7 else None), n=(1 if ic == 7 else None))
            snap_mm = C["pe"]
            wait("scalar", "pe", snap_mm)
            idx = a * 8 + kc
            if idx >= 2:
                wait("scalar", f"g{kc % 2}", kst_spill[idx - 2])
            on("scalar", "activation", kst[kc % 2][:], bank[:], AF.Identity,
               bias=bqk_c[:, 8 + kc:9 + kc], inc="ac", n=1)
            if idx >= 2:
                wait("scalar", "pe", ksq_use.get(("nmm", a, kc - 2), 0))
            on("scalar", "activation", ksq[kc % 2][:], bank[:], AF.Square,
               bias=bqk_c[:, 8 + kc:9 + kc], inc="ac", n=1)
            ksq_use[("evac", a, kc)] = C["ac"]
            snap_evac = C["ac"]
            wait("gpsimd", "ac", snap_evac - 1)
            dma("gpsimd", k_sp[kc * 128:(kc + 1) * 128, a * 512:(a + 1) * 512], kst[kc % 2][:], f"g{kc % 2}")
            kst_spill[idx] = C[f"g{kc % 2}"]

            def norm_mm(kcn, idxn):
                wait("tensor", "ac", ksq_use[("evac", a, kcn)])
                if idxn >= 1:
                    wait("tensor", "dv", kn_red[idxn - 1])
                on("tensor", "matmul", pf[2][:], ohm[:], ksq[kcn % 2][:], start=True, stop=True, inc="pe", n=1)
                ksq_use[("nmm", a, kcn)] = C["pe"]
                wait("vector", "pe", C["pe"])
                on("vector", "tensor_reduce", kn_part[0:2, kcn * 8 + a: kcn * 8 + a + 1], pf[2][0:2, :], AX.X, ALU.max, inc="dv", n=1)
                kn_red[idxn] = C["dv"]
            if kc == 3:
                flush_sqrt()
            if kc >= 1:
                norm_mm(kc - 1, idx - 1)
            if kc == 7:
                norm_mm(7, idx)
        if a == 0:
            wait("tensor", "sg", stage_v)
        for t in range(4):
            banks = (pf[4], pf[5]) if t % 2 == 0 else (pf[0], pf[1])
            vidx = a * 4 + t
            if t % 2 == 1:
                wait("tensor", "ac", ksq_use[("evac", a, 7)])
            if vidx >= 2:
                wait("tensor", "dv", vstage_ready[vidx - 2])
            for half in range(2):
                bank = banks[half]
                on("tensor", "matmul", bank[:], e0m[:], bv_pad[:, half * 512:(half + 1) * 512], start=True, stop=False)
                for ic in range(8):
                    on("tensor", "matmul", bank[:],
                       xn[:, ic, t * 128:(t + 1) * 128],
                       wv_bf[:, ic, half * 512:(half + 1) * 512],
                       start=False, stop=(ic == 7),
                       inc=("pe" if ic == 7 else None), n=(1 if ic == 7 else None))
            snap_vmm = C["pe"]
            wait("vector", "pe", snap_vmm)
            if vidx >= 2:
                wait("vector", f"g{2 + t % 2}", vstage_spill[vidx - 2])
            for half in range(2):
                on("vector", "tensor_copy",
                   vstage[t % 2][:].rearrange("p (h c) -> p h c", c=65)[:, half * 8:half * 8 + 8, 0:64],
                   banks[half][:].rearrange("p (h d) -> p h d", d=64),
                   inc=("dv" if half == 1 else None), n=(1 if half == 1 else None))
            vstage_ready[vidx] = C["dv"]
            wait("gpsimd", "dv", vstage_ready[vidx])
            dma("gpsimd", v_sp[a * 512 + t * 128: a * 512 + (t + 1) * 128, :], vstage[t % 2][:], f"g{2 + t % 2}")
            vstage_spill[vidx] = C[f"g{2 + t % 2}"]

    def emit_q_body(b, xn, xnrdy):
        wait("tensor", "dv", xnrdy)
        if b == 0:
            wait("tensor", "sg", stage_q)
        for qc in range(8):
            bank = pf[qc % 2]
            idx = b * 8 + qc
            if qc >= 2:
                wait("tensor", "ac", ksq_use.get(("qevac", b, qc - 2), 0))
            elif b == 0 and qc < 2:
                wait("tensor", "dv", vstage_ready[NBLK_A * 4 - 1])
            for ic in range(8):
                on("tensor", "matmul", bank[:],
                   wqk_bf[:, ic, qc * 128:(qc + 1) * 128],
                   xn[:, ic, :], start=(ic == 0), stop=(ic == 7),
                   inc=("pe" if ic == 7 else None), n=(1 if ic == 7 else None))
            snap_mm = C["pe"]
            wait("scalar", "pe", snap_mm)
            on("scalar", "activation", q_fm[:, qc, b * 512:(b + 1) * 512], bank[:],
               AF.Identity, bias=bqk_c[:, qc:qc + 1], inc="ac", n=1)
            if idx >= 2:
                wait("scalar", "pe", ksq_use.get(("qnmm", b, qc - 2), 0))
            on("scalar", "activation", ksq[qc % 2][:], bank[:], AF.Square,
               bias=bqk_c[:, qc:qc + 1], inc="ac", n=1)
            ksq_use[("qevac", b, qc)] = C["ac"]

            def qnorm_mm(qcn, idxn):
                wait("tensor", "ac", ksq_use[("qevac", b, qcn)])
                if idxn >= 1:
                    wait("tensor", "dv", qn_red[idxn - 1])
                on("tensor", "matmul", pf[2][:], ohm[:], ksq[qcn % 2][:], start=True, stop=True, inc="pe", n=1)
                ksq_use[("qnmm", b, qcn)] = C["pe"]
                wait("vector", "pe", C["pe"])
                on("vector", "tensor_reduce", qn_part[0:2, qcn * 4 + b: qcn * 4 + b + 1], pf[2][0:2, :], AX.X, ALU.max, inc="dv", n=1)
                qn_red[idxn] = C["dv"]
            if qc == 3:
                flush_sqrt()
            if qc >= 1:
                qnorm_mm(qc - 1, idx - 1)
            if qc == 7:
                qnorm_mm(7, idx)

    NALL = NBLK_A + NBLK_B
    binfo = [("A", i) for i in range(NBLK_A)] + [("B", i) for i in range(NBLK_B)]
    snb = {}
    for i in range(NALL + 1):
        if i < NALL:
            ph, idx = binfo[i]
            xb = xbuf[i % 2]
            if i >= 2:
                wait("sync", "ac", xa_lastuse[i - 2])
                src_ap = (x_kv[idx * 512:(idx + 1) * 512, :] if ph == "A"
                          else x_q[idx * 512:(idx + 1) * 512, :])
                dma("sync", xb[:], src_ap.rearrange("(t p) d -> p t d", p=128), f"d{i % 2}")
                snap_x = C[f"d{i % 2}"]
            else:
                snap_x = pre_snap[i]
            snb[i] = emit_ln_stats(xb, f"d{i % 2}", snap_x, i % 2)
        if i >= 1:
            ph1, idx1 = binfo[i - 1]
            if ph1 == "A":
                emit_kv_body(idx1, xn_fm[(i - 1) % 2], xn_ready[i - 1])
            else:
                emit_q_body(idx1, xn_fm[(i - 1) % 2], xn_ready[i - 1])
        if i < NALL:
            xn_ready[i], xa_lastuse[i] = emit_ln_finish(xbuf[i % 2], xn_tm[i % 2], xn_fm[i % 2], i % 2)
    barrier()
    es_ab.close()
    es_ps.close()

    es_c = ExitStack()
    scb = lambda name, shape, dt: es_c.enter_context(nc.sbuf_tensor(name, shape, dt))
    psc = [es_c.enter_context(nc.psum_tensor(f"psc{i}", [128, 1024], F32)) for i in range(2)]
    pav = [es_c.enter_context(nc.psum_tensor(f"pav{i}", [128, 512], F32)) for i in range(4)]
    vb = [scb(f"vb{i}", [128, 32, 130], BF16) for i in range(3)]
    kb = [scb(f"kb{i}", [128, 4096], BF16) for i in range(3)]
    ptb = scb("ptb", [128, 8, 1024], BF16)

    for gs in ("g0", "g1", "g2", "g3"):
        wait("scalar", gs, C[gs])
    wait("tensor", "dg", gm_snap)
    on("vector", "tensor_reduce", kmax[:], kn_part[:].rearrange("p (h a) -> p h a", a=8), AX.X, ALU.max)
    on("vector", "drain")
    for g_ in range(NGRP):
        on("vector", "tensor_mul", m2[:, g_ * 8:(g_ + 1) * 8],
           qn_part[:].rearrange("p (h g) -> p h g", g=4)[:, :, g_], kmax[:])
    on("vector", "drain")
    on("vector", "nop", inc="dv", n=1)
    wait("scalar", "dv", C["dv"])
    on("scalar", "activation", msm[:], m2[:], AF.Sqrt, scale=1.0 / 64.0, inc="ac", n=1)
    wait("tensor", "ac", C["ac"])
    on("tensor", "matmul", psc[0][:, 0:32], sel0[:], msm[:], start=True, stop=True)
    on("tensor", "matmul", psc[0][:, 32:64], sel1[:], msm[:], start=True, stop=True, inc="pe", n=1)
    wait("scalar", "pe", C["pe"])
    on("scalar", "activation", mb[:], psc[0][:, 0:64], AF.Identity, scale=-1.0, inc="ac", n=1)
    on("scalar", "drain")
    snapm = C["ac"]
    wait("vector", "ac", snapm)
    on("vector", "tensor_tensor", mbp[:], mb[:, 0:32], mb[:, 32:64], ALU.min, inc="dv", n=1)
    wait("scalar", "dv", C["dv"])
    mb_ready = C["ac"]

    kb_lastuse = {}
    vb_lastuse = {}
    pending_tail = [None]
    last_ao = [0]
    kdma_snap = {}
    vdma_snap = {}
    av_snap = {}
    KSEM = ["d4", "d5", "d2"]
    VSEM = ["d6", "d7", "d3"]
    semA_last = {0: 0, 1: 0, 2: 0}
    semB_last = {0: 0, 1: 0, 2: 0}

    def emit_kv_load(n):
        if n >= NGRP * 8:
            return
        gl, hl = n // 8, n % 8
        El = 8 * gl + 8
        KEl = El * 128
        wait("sync", "pe", semA_last[n % 3])
        wait("sync", f"g{hl % 2}", kst_spill[(2 * gl + 1) * 8 + hl])
        dma("sync", kb[n % 3][:, 0:KEl], k_sp[hl * 128:(hl + 1) * 128, 0:KEl], KSEM[n % 3])
        kdma_snap[n] = C[KSEM[n % 3]]
        wait("sync", "pe", semB_last[n % 3])
        wait("sync", "g2", vstage_spill[(2 * gl + 1) * 4 + 2])
        wait("sync", "g3", vstage_spill[(2 * gl + 1) * 4 + 3])
        dma("sync", vb[n % 3][:, 0:El, :], v_sp[0:KEl, 130 * hl:130 * (hl + 1)].rearrange("(e p) c -> p e c", p=128), VSEM[n % 3])
        vdma_snap[n] = C[VSEM[n % 3]]
    exp_done = {}
    av_done = {}
    stt_done = {}
    x2b_spill = {}

    emit_kv_load(0)
    emit_kv_load(1)
    snb_d = {}
    xn_rdy_d = {}
    out_dma = {}

    def emit_d_load_stats(b):
        xb = xbuf[(b + 1) % 2]
        if b == 0:
            wait("sync", "g4", 32)
        if b == 1:
            wait("sync", "dv", stt_done[(3, 7, 1)])
        wait("sync", f"g{4 + b % 2}", 16 * (b // 2 + 1))
        if b >= 2:
            wait("sync", f"g{6 + b % 2}", out_dma[b - 2])
        dma("sync", xb[:], x2_sp[b * 512:(b + 1) * 512, :].rearrange("(t p) d -> p t d", p=128), f"d{(b + 1) % 2}")
        snap_x = C[f"d{(b + 1) % 2}"]
        snb_d[b] = emit_ln_stats(xb, f"d{(b + 1) % 2}", snap_x, b % 2)

    for g in range(NGRP):
        if g == 3:
            emit_d_load_stats(0)
        E = 8 * g + 8
        KE = E * 128
        xq = xbuf[0]
        if g >= 1:
            wait("sync", "dv", stt_done[(g - 1, 7, 1)])
        dma("sync", xq[:], x_q[g * 512:(g + 1) * 512, :].rearrange("(t p) d -> p t d", p=128), "d0")
        xq_snap = C["d0"]
        x2b = xbuf[1 + g % 2]
        if g >= 2:
            wait("vector", f"g{4 + g % 2}", x2b_spill[g - 2])
        for hp in range(8):
            h0, h1 = 2 * hp, 2 * hp + 1
            if (g, hp) != (0, 0):
                pass
            n_lin = g * 8 + hp
            emit_kv_load(n_lin + 2)
            kdma = kdma_snap[n_lin]
            vdma = vdma_snap[n_lin]
            wait("tensor", KSEM[(g * 8 + hp) % 3], kdma)
            wait("tensor", "ac", mb_ready)
            if g >= 1 or hp >= 1:
                prevpair = (g, hp - 1) if hp >= 1 else (g - 1, 7)
                wait("scalar", "pe", av_done.get(prevpair, 0))
            prevpair = (g, hp - 1) if hp >= 1 else (g - 1, 7)
            wait("tensor", VSEM[(g * 8 + hp) % 3], vdma)

            def av_mms(c, stop):
                on("tensor", "matmul", pav[0][0:65, :], vb[(g * 8 + hp) % 3][:, c, 0:65], ptb[:, c % 8, 0:512], start=(c == 0), stop=stop)
                on("tensor", "matmul", pav[1][0:65, :], vb[(g * 8 + hp) % 3][:, c, 65:130], ptb[:, c % 8, 512:1024], start=(c == 0), stop=stop,
                   inc="pe", n=1)
                av_snap[(g, hp, c)] = C["pe"]

            for c in range(E):
                bank = psc[c % 2]
                bA = bank[:, 0:512]
                bB = bank[:, 512:1024]
                if c >= 2:
                    wait("tensor", "ac", exp_done[(g, hp, c - 2)])
                elif g > 0 or hp > 0:
                    wait("tensor", "ac", exp_done[(prevpair[0], prevpair[1], 8 * prevpair[0] + 8 - 1 - (1 - c))])
                masked = c >= 8 * g
                rel = c - 8 * g
                w = 128 * (rel // 2 + 1) if masked else 512
                kbt = kb[(g * 8 + hp) % 3]
                if masked and w < 512:
                    for rt, bb in ((0, bA), (64, bB)):
                        on("tensor", "matmul", bb[:, w:512], kbt[rt:rt + 64, c * 128:(c + 1) * 128],
                           q_fm[rt:rt + 64, hp, g * 512 + w:(g + 1) * 512], start=True, stop=True)
                        on("tensor", "matmul", bb[:, 0:w], kbt[rt:rt + 64, c * 128:(c + 1) * 128],
                           q_fm[rt:rt + 64, hp, g * 512:g * 512 + w], start=True, stop=False)
                    on("tensor", "matmul", bA[:, 0:w], hm[:], gm[:, rel, 0:w], start=False, stop=True)
                    on("tensor", "matmul", bB[:, 0:w], hm[:], gm[:, rel, 0:w], start=False, stop=True, inc="pe", n=1)
                else:
                    on("tensor", "matmul", bA, kbt[0:64, c * 128:(c + 1) * 128],
                       q_fm[0:64, hp, g * 512:(g + 1) * 512], start=True, stop=not masked)
                    on("tensor", "matmul", bB, kbt[64:128, c * 128:(c + 1) * 128],
                       q_fm[64:128, hp, g * 512:(g + 1) * 512], start=True, stop=not masked,
                       inc=(None if masked else "pe"), n=(None if masked else 1))
                    if masked:
                        on("tensor", "matmul", bA, hm[:], gm[:, rel, :], start=False, stop=True)
                        on("tensor", "matmul", bB, hm[:], gm[:, rel, :], start=False, stop=True, inc="pe", n=1)
                snap_s = C["pe"]
                wait("scalar", "pe", snap_s)
                if c >= 8:
                    wait("scalar", "pe", av_snap[(g, hp, c - 8)])
                on("scalar", "activation", ptb[:, c % 8, :], bank[:], AF.Exp, scale=0.125,
                   bias=mbp[:, g * 8 + hp:g * 8 + hp + 1], inc="ac", n=1)
                exp_done[(g, hp, c)] = C["ac"]
                if c == 3 and pending_tail[0] is not None:
                    pending_tail[0]()
                    pending_tail[0] = None
                if c >= 2:
                    if c == 2 and (g > 0 or hp > 0):
                        wait("tensor", "dv", last_ao[0])
                    wait("tensor", "ac", exp_done[(g, hp, c - 2)])
                    av_mms(c - 2, False)
            wait("tensor", "ac", exp_done[(g, hp, E - 2)])
            av_mms(E - 2, False)
            wait("tensor", "ac", exp_done[(g, hp, E - 1)])
            av_mms(E - 1, True)
            av_done[(g, hp)] = C["pe"]
            semA_last[(g * 8 + hp) % 3] = C["pe"]
            semB_last[(g * 8 + hp) % 3] = C["pe"]
            wait("vector", "pe", av_done[(g, hp)])
            for pi in range(2):
                on("vector", "tensor_copy", ao[pi][:], pav[pi][0:65, :],
                   inc=("dv" if pi == 1 else None), n=(1 if pi == 1 else None))
            ao_snap = C["dv"]

            def attn_tail(g=g, hp=hp, ao_snap=ao_snap, xq=xq, x2b=x2b, xq_snap=xq_snap):
                for pi in range(2):
                    h = 2 * hp + pi
                    psT = pav[2 + pi]
                    wait("tensor", "dv", ao_snap)
                    if (g, hp) != (0, 0):
                        pv = (g, hp - 1) if hp >= 1 else (g - 1, 7)
                        wait("tensor", "dv", stt_done.get((pv[0], pv[1], pi), 0))
                    for t in range(4):
                        on("tensor", "transpose", psT[:, t * 65:t * 65 + 65], ao[pi][0:65, t * 128:(t + 1) * 128], identf[0:65, 0:65],
                           inc=("pe" if t == 3 else None), n=(1 if t == 3 else None))
                    snap_tr = C["pe"]
                    wait("vector", "pe", snap_tr)
                    wait("vector", "d0", xq_snap)
                    on("vector", "tensor_copy", zc[:], psT[:, 0:260].rearrange("p (t c) -> p t c", c=65)[:, :, 64])
                    on("vector", "drain")
                    on("vector", "reciprocal", zr[:], zc[:])
                    on("vector", "drain")
                    for t in range(4):
                        on("vector", "scalar_tensor_tensor", x2b[:, t, h * 64:(h + 1) * 64],
                           psT[:, t * 65:t * 65 + 64], zr[:, t:t + 1], xq[:, t, h * 64:(h + 1) * 64],
                           ALU.mult, ALU.add, inc=("dv" if t == 3 else None), n=(1 if t == 3 else None))
                    stt_done[(g, hp, pi)] = C["dv"]
            pending_tail[0] = attn_tail
            last_ao[0] = ao_snap
        if pending_tail[0] is not None:
            pending_tail[0]()
            pending_tail[0] = None
        wait("gpsimd", "dv", stt_done[(g, 7, 1)])
        dma("gpsimd", x2_sp[g * 512:(g + 1) * 512, :].rearrange("(t p) d -> p t d", p=128), x2b[:], f"g{4 + g % 2}")
        x2b_spill[g] = C[f"g{4 + g % 2}"]
    barrier()
    es_c.close()
    es_q.close()

    es_d = ExitStack()
    sdb = lambda name, shape, dt: es_d.enter_context(nc.sbuf_tensor(name, shape, dt))
    xn2_tm = [sdb(f"xn2tm{i}", [128, 4, 1024], BF16) for i in range(2)]
    xn2_fm = [sdb(f"xn2fm{i}", [128, 8, 512], BF16) for i in range(2)]
    wb12 = [sdb(f"wb12_{i}", [128, 2, 8, 128], BF16) for i in range(3)]
    wbh = [sdb(f"wbh{i}", [128, 32, 128], BF16) for i in range(3)]
    su = [sdb(f"su{i}", [128, 512], BF16) for i in range(2)]
    sv2 = [sdb(f"sv2_{i}", [128, 512], BF16) for i in range(2)]
    abuf = sdb("abuf", [128, 32, 512], BF16)
    bbuf = sdb("bbuf", [128, 32, 512], BF16)
    ytmp = [sdb(f"ytmp{i}", [128, 512], F32) for i in range(2)]

    x2_lastuse = {}
    xn2_lastuse = {}
    ev_done = {}
    mul_done = {}
    w_lastuse = {}
    y_add_done = {}

    if wmirror:
        wait("sync", "gm0", phasew_gm)

    def emit_ffn_body(b):
        xb = xbuf[(b + 1) % 2]
        xn = xn2_fm[b % 2]
        wait("tensor", "dv", xn_rdy_d[b])
        for oc in range(32):
            widx = b * 32 + oc
            wbuf = widx % 3
            wait("sync", "pe", semA_last[wbuf])
            dma("sync", wb12[wbuf][:, 0, :, :], wsrc["w1"][oc], KSEM[wbuf])
            dma("sync", wb12[wbuf][:, 1, :, :], wsrc["w2"][oc], KSEM[wbuf])
            wdma = C[KSEM[wbuf]]
            bA = pf[2 * (oc % 2)]
            bB = pf[2 * (oc % 2) + 1]
            wait("tensor", KSEM[wbuf], wdma)
            if oc >= 2:
                wait("tensor", "ac", ev_done.get(("w12", widx - 2), 0))
            elif b >= 1:
                wait("tensor", "ac", ev_done[("w3", b * 8 - 1)])
                wait("tensor", "dv", y_add_done[("add", b * 8 - 1)])
            for ic in range(8):
                on("tensor", "matmul", bA[:], wb12[wbuf][:, 0, ic, :], xn[:, ic, :], start=(ic == 0), stop=(ic == 7))
            for ic in range(8):
                on("tensor", "matmul", bB[:], wb12[wbuf][:, 1, ic, :], xn[:, ic, :], start=(ic == 0), stop=(ic == 7),
                   inc=("pe" if ic == 7 else None), n=(1 if ic == 7 else None))
            w_lastuse[("w12", widx)] = C["pe"]
            semA_last[wbuf] = C["pe"]
            snap_mm = C["pe"]
            wait("scalar", "pe", snap_mm)
            if oc >= 2 or b >= 1:
                wait("scalar", "dv", mul_done.get(widx - 2, 0))
            on("scalar", "activation", su[oc % 2][:], bA[:], AF.Silu, scale=scc["s1"][:, oc:oc + 1])
            on("scalar", "activation", sv2[oc % 2][:], bB[:], AF.Identity, scale=scc["s2"][:, oc:oc + 1], inc="ac", n=1)
            ev_done[("w12", widx)] = C["ac"]
            wait("vector", "ac", C["ac"])
            on("vector", "tensor_mul", abuf[:, oc, :], su[oc % 2][:], sv2[oc % 2][:], inc="dv", n=1)
            mul_done[widx] = C["dv"]
            if oc == 3:
                flush_sqrt()
        wait("tensor", "dv", mul_done[b * 32 + 31])
        for oc in range(32):
            widx = b * 32 + oc
            wbuf = widx % 3
            wait("sync", "pe", semB_last[wbuf])
            dma("sync", wbh[wbuf][:], wsrc["h1"][oc], VSEM[wbuf])
            wdma = C[VSEM[wbuf]]
            bank = pf[oc % 2]
            wait("tensor", VSEM[wbuf], wdma)
            if oc >= 2 or b >= 1:
                wait("tensor", "ac", ev_done.get(("h1", widx - 2), 0))
            elif b == 0:
                wait("tensor", "ac", ev_done[("w12", 31)])
            for ic in range(32):
                on("tensor", "matmul", bank[:], wbh[wbuf][:, ic, :], abuf[:, ic, :], start=(ic == 0), stop=(ic == 31),
                   inc=("pe" if ic == 31 else None), n=(1 if ic == 31 else None))
            w_lastuse[("h1", widx)] = C["pe"]
            semB_last[wbuf] = C["pe"]
            wait("scalar", "pe", C["pe"])
            on("scalar", "activation", bbuf[:, oc, :], bank[:], AF.Gelu, scale=scc["sh1"][:, oc:oc + 1], inc="ac", n=1)
            ev_done[("h1", widx)] = C["ac"]
        wait("tensor", "ac", ev_done[("h1", b * 32 + 31)])
        for oc in range(32):
            widx = b * 32 + oc
            wbuf = widx % 3
            wait("sync", "pe", semB_last[wbuf])
            dma("sync", wbh[wbuf][:], wsrc["h2"][oc], VSEM[wbuf])
            wdma = C[VSEM[wbuf]]
            bank = pf[oc % 2]
            wait("tensor", VSEM[wbuf], wdma)
            if oc >= 2 or b >= 1:
                wait("tensor", "ac", ev_done.get(("h2", widx - 2), 0))
            for ic in range(32):
                on("tensor", "matmul", bank[:], wbh[wbuf][:, ic, :], bbuf[:, ic, :], start=(ic == 0), stop=(ic == 31),
                   inc=("pe" if ic == 31 else None), n=(1 if ic == 31 else None))
            w_lastuse[("h2", widx)] = C["pe"]
            semB_last[wbuf] = C["pe"]
            wait("scalar", "pe", C["pe"])
            on("scalar", "activation", abuf[:, oc, :], bank[:], AF.Gelu, scale=scc["sh2"][:, oc:oc + 1], inc="ac", n=1)
            ev_done[("h2", widx)] = C["ac"]
        wait("tensor", "ac", ev_done[("h2", b * 32 + 31)])
        for oc in range(8):
            widx = b * 8 + oc
            wbuf = (b * 8 + oc) % 3
            wait("sync", "pe", semB_last[wbuf])
            dma("sync", wbh[wbuf][:], wsrc["w3"][oc], VSEM[wbuf])
            wdma = C[VSEM[wbuf]]
            bank = pf[oc % 2]
            wait("tensor", VSEM[wbuf], wdma)
            if oc >= 2 or b >= 1:
                wait("tensor", "ac", ev_done.get(("w3", widx - 2), 0))
            for ic in range(32):
                on("tensor", "matmul", bank[:], wbh[wbuf][:, ic, :], abuf[:, ic, :], start=(ic == 0), stop=(ic == 31),
                   inc=("pe" if ic == 31 else None), n=(1 if ic == 31 else None))
            w_lastuse[("w3", widx)] = C["pe"]
            semB_last[wbuf] = C["pe"]
            wait("scalar", "pe", C["pe"])
            if oc >= 2 or b >= 1:
                wait("scalar", "pe", y_add_done.get(("tr", widx - 2), 0))
            on("scalar", "activation", ytmp[oc % 2][:], bank[:], AF.Identity, scale=scc["s3"][:, oc:oc + 1], inc="ac", n=1)
            ev_done[("w3", widx)] = C["ac"]

            def y_chain(ocn, widxn):
                yb = pf[4 + ocn % 2]
                wait("tensor", "ac", ev_done[("w3", widxn)])
                if ocn >= 2 or b >= 1:
                    wait("tensor", "dv", y_add_done.get(("add", widxn - 2), 0))
                for t in range(4):
                    on("tensor", "transpose", yb[:, t * 128:(t + 1) * 128], ytmp[ocn % 2][:, t * 128:(t + 1) * 128], identf[:],
                       inc=("pe" if t == 3 else None), n=(1 if t == 3 else None))
                y_add_done[("tr", widxn)] = C["pe"]
                wait("vector", "pe", C["pe"])
                for t in range(4):
                    on("vector", "tensor_tensor", xb[:, t, ocn * 128:(ocn + 1) * 128], yb[:, t * 128:(t + 1) * 128],
                       xb[:, t, ocn * 128:(ocn + 1) * 128], ALU.add,
                       inc=("dv" if t == 3 else None), n=(1 if t == 3 else None))
                y_add_done[("add", widxn)] = C["dv"]
            if oc >= 1:
                y_chain(oc - 1, widx - 1)
            if oc == 7:
                y_chain(7, widx)
        xn2_lastuse[b] = w_lastuse[("w12", b * 32 + 31)]
        x2_lastuse[b] = C["dv"]
        wait("gpsimd", "dv", y_add_done[("add", b * 8 + 7)])
        for t in range(4):
            dma("gpsimd", out_d[b * 512 + t * 128: b * 512 + (t + 1) * 128, :], xb[:, t, :], f"g{6 + b % 2}")
        out_dma[b] = C[f"g{6 + b % 2}"]

    for i in range(NBLK_D + 1):
        if 0 < i < NBLK_D:
            emit_d_load_stats(i)
        if i >= 1:
            emit_ffn_body(i - 1)
        if i < NBLK_D:
            xn_rdy_d[i], _apl = emit_ln_finish(xbuf[(i + 1) % 2], xn2_tm[i % 2], xn2_fm[i % 2], i % 2)

    wait("sync", "g6", C["g6"])
    wait("sync", "g7", C["g7"])
    wait("gpsimd", "g6", C["g6"])
    wait("gpsimd", "g7", C["g7"])

    with nc.Block() as blk:
        @blk.sync
        def _(g):
            for f in EV["sync"]:
                f(g)

        @blk.gpsimd
        def _(g):
            for f in EV["gpsimd"]:
                f(g)

        @blk.tensor
        def _(g):
            for f in EV["tensor"]:
                f(g)

        @blk.vector
        def _(g):
            for f in EV["vector"]:
                f(g)

        @blk.scalar
        def _(g):
            for f in EV["scalar"]:
                f(g)

    es_d.close()
    es.close()
    return nc



def _tile_w(w):
    o, i = w.shape
    wt = np.ascontiguousarray(w.T)
    t = wt.reshape(i // 128, 128, o // 128, 128)
    return np.ascontiguousarray(t.transpose(2, 1, 0, 3))


def _maybe_bf16(a):
    ab = a.astype(ml_dtypes.bfloat16)
    if (ab.astype(np.float32) == a).all():
        return ab, True
    return a, False


def _host_prep(inp):
    x = np.asarray(inp["L0_attn_hidden_states"], np.float32)
    wqk = np.asarray(inp["L0_attn_Wqk"], np.float32)
    bqk = np.asarray(inp["L0_attn_bqk"], np.float32)
    wv = np.asarray(inp["L0_attn_Wv"], np.float32)
    bv = np.asarray(inp["L0_attn_bv"], np.float32)
    ws = {}
    flags = {}
    for nm, key in (("w1", "L0_ffn_w1_w"), ("w2", "L0_ffn_w2_w"), ("h1", "L0_ffn_h1_w"),
                    ("h2", "L0_ffn_h2_w"), ("w3", "L0_ffn_w3_w")):
        t = _tile_w(np.asarray(inp[key], np.float32))
        t2, ok = _maybe_bf16(t)
        ws[nm] = t2
        flags[nm] = ok
    scales = {"s1": inp["L0_ffn_w1_s"], "s2": inp["L0_ffn_w2_s"], "sh1": inp["L0_ffn_h1_s"],
              "sh2": inp["L0_ffn_h2_s"], "s3": inp["L0_ffn_w3_s"]}

    ident = np.eye(128, dtype=ml_dtypes.bfloat16)
    identf = np.eye(128, dtype=np.float32)
    d = np.arange(128)
    hmat = (d[None, :] >= d[:, None]).astype(ml_dtypes.bfloat16)

    def build_g(cp):
        gmk = np.zeros((8, 128, 512), np.float32)
        for r in range(8):
            for t in range(4):
                dstate = 2 * t + cp
                if r == dstate:
                    for qr in range(127):
                        gmk[r, qr + 1, 128 * t + qr] = MASKB
                elif r > dstate:
                    gmk[r, 0, 128 * t:128 * t + 128] = MASKB
        return gmk.astype(ml_dtypes.bfloat16)

    gms = [build_g(0), build_g(1)]
    in_maps = []
    for core in range(8):
        b, cp = core >> 1, core & 1
        xb = x[b]
        xq = np.ascontiguousarray(xb.reshape(32, 128, D)[cp::2].reshape(2048, D))
        m = {
            "x_kv": xb, "x_q": xq,
            "wqk": wqk, "bqk": np.ascontiguousarray(bqk.reshape(16, 128).T),
            "wv": wv, "bv": bv.reshape(1, D),
            "ident": ident, "identf": identf, "hm": hmat, "gm": gms[cp],
        }
        for nm in ws:
            m["w_" + nm] = ws[nm]
        for nm in ("s1", "s2", "sh1", "sh2", "s3"):
            sc = np.asarray(scales[nm], np.float32)
            m[nm] = np.ascontiguousarray(sc.reshape(-1, 128).T)
        in_maps.append(m)
    return flags, in_maps


def kernel(**inp):
    flags, in_maps = _host_prep(inp)
    nc = _build(flags)
    res = run_bass_kernel_spmd(nc, in_maps, core_ids=list(range(8)))
    out = np.empty((B, S, D), np.float32)
    for core in range(8):
        b, cp = core >> 1, core & 1
        oc = res.results[core]["out"].reshape(16, 128, D)
        out.reshape(B, 32, 128, D)[b, cp::2] = oc
    return out


# version 26
# speedup vs baseline: 1.0353x; 1.0201x over previous
import numpy as np
import ml_dtypes
import concourse.bass as bass
import concourse.mybir as mybir
from concourse.bass_utils import run_bass_kernel_spmd
from contextlib import ExitStack

F32 = mybir.dt.float32
BF16 = mybir.dt.bfloat16
AF = mybir.ActivationFunctionType
ALU = mybir.AluOpType
AX = mybir.AxisListType

B, S, D = 4, 4096, 1024
NH, HD = 16, 64
NT = S // 128
OWN = NT // 2
NBLK_A = S // 512
NBLK_B = 2048 // 512
NGRP = 4
NBLK_D = 2048 // 512
DH = 4096
EPS = 1e-5
MASKB = -32768.0

ENGS = ("sync", "gpsimd", "tensor", "vector", "scalar")


def _build(ffn_bf16_flags):
    ffn_all_bf16 = all(ffn_bf16_flags.values())
    nc = bass.Bass("TRN2", target_bir_lowering=False)

    x_kv = nc.dram_tensor("x_kv", [S, D], F32, kind="ExternalInput").ap()
    x_q = nc.dram_tensor("x_q", [2048, D], F32, kind="ExternalInput").ap()
    wqk_d = nc.dram_tensor("wqk", [D, 2 * D], F32, kind="ExternalInput").ap()
    bqk_d = nc.dram_tensor("bqk", [128, 16], F32, kind="ExternalInput").ap()
    wv_d = nc.dram_tensor("wv", [D, D], F32, kind="ExternalInput").ap()
    bv_d = nc.dram_tensor("bv", [1, D], F32, kind="ExternalInput").ap()
    ident_d = nc.dram_tensor("ident", [128, 128], BF16, kind="ExternalInput").ap()
    identf_d = nc.dram_tensor("identf", [128, 128], F32, kind="ExternalInput").ap()
    hm_d = nc.dram_tensor("hm", [128, 128], BF16, kind="ExternalInput").ap()
    gm_d = nc.dram_tensor("gm", [8, 128, 512], BF16, kind="ExternalInput").ap()
    wspec = {"w1": 32, "w2": 32, "h1": (32, 32), "h2": (32, 32), "w3": (8, 32)}
    wdim = {"w1": (32, 8), "w2": (32, 8), "h1": (32, 32), "h2": (32, 32), "w3": (8, 32)}
    wdram = {}
    wmirror = {}
    for nm, (no, ni) in wdim.items():
        dt = BF16 if ffn_bf16_flags[nm] else F32
        wdram[nm] = nc.dram_tensor("w_" + nm, [no, 128, ni, 128], dt, kind="ExternalInput").ap()
        if not ffn_bf16_flags[nm]:
            wmirror[nm] = nc.dram_tensor("wm_" + nm, [no, 128, ni, 128], BF16, kind="Internal").ap()
    sc_d = {}
    for nm, w in (("s1", 32), ("s2", 32), ("sh1", 32), ("sh2", 32), ("s3", 8)):
        sc_d[nm] = nc.dram_tensor(nm, [128, w], F32, kind="ExternalInput").ap()
    k_sp = nc.dram_tensor("k_sp", [D, S], BF16, kind="Internal").ap()
    v_sp = nc.dram_tensor("v_sp", [S, 1040], BF16, kind="Internal").ap()
    x2_sp = nc.dram_tensor("x2_sp", [2048, D], F32, kind="Internal").ap()
    out_d = nc.dram_tensor("out", [2048, D], F32, kind="ExternalOutput").ap()

    es = ExitStack()
    sb = lambda name, shape, dt: es.enter_context(nc.sbuf_tensor(name, shape, dt))
    ident = sb("identb", [128, 128], BF16)
    identf = sb("identfb", [128, 128], F32)
    hm = sb("hmb", [128, 128], BF16)
    gm = sb("gmb", [128, 8, 512], BF16)
    ones_col = sb("ones_col", [1, 128], BF16)
    e0m = sb("e0m", [128, 128], BF16)
    ohm = sb("ohm", [128, 128], BF16)
    sel0 = sb("sel0", [2, 128], BF16)
    sel1 = sb("sel1", [2, 128], BF16)
    bv_pad = sb("bv_pad", [128, 1024], BF16)
    bqk_c = sb("bqk_c", [128, 16], F32)
    scc = {nm: sb("c_" + nm, [128, w], F32) for nm, w in
           (("s1", 32), ("s2", 32), ("sh1", 32), ("sh2", 32), ("s3", 8))}
    kn_part = sb("kn_part", [2, 64], F32)
    qn_part = sb("qn_part", [2, 32], F32)
    kmax = sb("kmax", [2, 8], F32)
    m2 = sb("m2", [2, 32], F32)
    msm = sb("msm", [2, 32], BF16)
    mb = sb("mb", [128, 64], F32)
    xbuf = [sb(f"xbuf{i}", [128, 4, 1024], F32) for i in range(3)]
    stats = [sb(f"stats{i}", [128, 48], F32) for i in range(2)]
    mv = [sb(f"mv{i}", [128, 8], F32) for i in range(2)]
    tmp4 = [sb(f"tmp4_{i}", [128, 4], F32) for i in range(2)]
    rec4 = [sb(f"rec4_{i}", [128, 4], F32) for i in range(2)]
    rstd4 = [sb(f"rstd4_{i}", [128, 4], F32) for i in range(2)]
    nb4 = [sb(f"nb4_{i}", [128, 4], F32) for i in range(2)]
    mbp = sb("mbp", [128, 32], F32)

    ao = [sb(f"ao{i}", [65, 512], F32) for i in range(2)]
    zc = sb("zc", [128, 4], F32)
    zr = sb("zr", [128, 4], F32)

    es_ps = ExitStack()
    pf = [es_ps.enter_context(nc.psum_tensor(f"pf{i}", [128, 512], F32)) for i in range(6)]
    pb = [es_ps.enter_context(nc.psum_tensor(f"pb{i}", [128, 1024], BF16)) for i in range(2)]

    semnames = (["dc", "dg", "sg", "sh", "sw"] + [f"d{i}" for i in range(8)] + [f"g{i}" for i in range(8)]
                + ["gm0", "pe", "dv", "ac", "ph"])
    sem = {n: es.enter_context(nc.semaphore("s_" + n)) for n in semnames}
    C = {n: 0 for n in sem}

    EV = {e: [] for e in ENGS}

    def on(e, meth, *args, inc=None, n=None, **kw):
        def f(g, meth=meth, args=args, kw=kw, inc=inc, n=n):
            ins = getattr(g, meth)(*args, **kw)
            if inc is not None:
                ins.then_inc(sem[inc], n)
        EV[e].append(f)
        if inc is not None:
            C[inc] += n

    def wait(e, s, v):
        if v > 0:
            EV[e].append(lambda g, s=s, v=v: g.wait_ge(sem[s], v))

    def dma(e, out, in_, inc):
        on(e, "dma_start", out=out, in_=in_, inc=inc, n=16)

    def barrier():
        for e in ENGS:
            on(e, "nop", inc="ph", n=1)
        tgt = C["ph"]
        for e in ENGS:
            wait(e, "ph", tgt)

    if not ffn_all_bf16:
        pass
    else:
        dma("sync", xbuf[0][:], x_kv[0:512, :].rearrange("(t p) d -> p t d", p=128), "d0")
        dma("sync", xbuf[1][:], x_kv[512:1024, :].rearrange("(t p) d -> p t d", p=128), "d1")
    dma("sync", ident[:], ident_d, "dc")
    dma("sync", identf[:], identf_d, "dc")
    dma("sync", hm[:], hm_d, "dc")
    dma("sync", bqk_c[:], bqk_d, "dc")
    for nm in scc:
        dma("sync", scc[nm][:], sc_d[nm], "dc")
    es_q = ExitStack()
    q_fm = es_q.enter_context(nc.sbuf_tensor("q_fm", [128, 8, 2048], BF16))
    bv_f = es_q.enter_context(nc.sbuf_tensor("bv_f", [1, D], F32))
    dma("sync", bv_f[:], bv_d, "dc")
    dc_total = C["dc"]
    wait("vector", "dc", dc_total)
    on("vector", "memset", ones_col[:], 1.0)
    on("vector", "memset", e0m[:], 0.0)
    on("vector", "memset", ohm[:], 0.0)
    on("vector", "memset", sel0[:], 0.0)
    on("vector", "memset", sel1[:], 1.0)
    on("vector", "memset", bv_pad[:], 0.0)
    on("vector", "drain")
    on("vector", "memset", e0m[0:1, :], 1.0)
    on("vector", "memset", ohm[0:64, 0:1], 1.0)
    on("vector", "memset", ohm[64:128, 1:2], 1.0)
    on("vector", "memset", sel0[0:1, :], 1.0)
    on("vector", "memset", sel1[0:1, :], 0.0)
    on("vector", "tensor_copy", bv_pad[0:1, :], bv_f[:], inc="dv", n=1)
    es_ab = ExitStack()
    sab = lambda name, shape, dt: es_ab.enter_context(nc.sbuf_tensor(name, shape, dt))
    wqk_bf = sab("wqk_bf", [128, 8, 2048], BF16)
    wv_bf = sab("wv_bf", [128, 8, 1024], BF16)
    xf = xbuf[0][:].rearrange("p a d -> p (a d)")
    xf2 = xbuf[1][:].rearrange("p a d -> p (a d)")
    xf3 = xbuf[2][:].rearrange("p a d -> p (a d)")
    pre_snap = {0: 16, 1: 16}
    dma("sync", gm[:], gm_d.rearrange("r p q -> p r q"), "dg")
    gm_snap = C["dg"]
    stage_jobs = []
    for ic in range(8):
        stage_jobs.append((wqk_d.rearrange("(i p) o -> p i o", p=128)[:, ic:ic + 1, 1024:2048],
                           wqk_bf[:, ic:ic + 1, 1024:2048], 1024, 1024))
    for half in range(4):
        stage_jobs.append((wv_d.rearrange("(i p) o -> p i o", p=128)[:, 2 * half:2 * half + 2, :],
                           wv_bf[:, 2 * half:2 * half + 2, :], 1024, 2048))
    for ic in range(8):
        stage_jobs.append((wqk_d.rearrange("(i p) o -> p i o", p=128)[:, ic:ic + 1, 0:1024],
                           wqk_bf[:, ic:ic + 1, 0:1024], 1024, 1024))
    for j, (src_ap, dst_ap, owid, nel) in enumerate(stage_jobs):
        hv = xf3[:, (j % 2) * 2048:(j % 2) * 2048 + nel].rearrange("p (i o) -> p i o", o=owid)
        ssem = "d2" if j % 2 == 0 else "sh"
        if j >= 2:
            wait("sync", "sg", j - 1)
        dma("sync", hv, src_ap, ssem)
        wait("gpsimd", ssem, C[ssem])
        on("gpsimd", "tensor_copy", dst_ap, hv, inc="sg", n=1)
    stage_ph = C["sg"]
    stage_k, stage_v, stage_q = stage_ph - 12, stage_ph - 8, stage_ph

    if wmirror:
        wait("sync", "sg", C["sg"])
    for nm in wmirror:
        no, ni = wdim[nm]
        for oc in range(no):
            wait("sync", "sw", C["sw"])
            dma("sync", xf3.rearrange("p (i o) -> p i o", o=128)[:, 0:ni, :],
                wdram[nm][oc], "d2")
            wait("vector", "d2", C["d2"])
            wait("vector", "gm0", C["gm0"])
            on("vector", "tensor_copy", xf2[:, 0:ni * 128], xf3[:, 0:ni * 128], inc="sw", n=1)
            wait("gpsimd", "sw", C["sw"])
            dma("gpsimd", wmirror[nm][oc],
                xf2.rearrange("p (i o) -> p i o", o=128)[:, 0:ni, :], "gm0")
    wsrc = {nm: (wmirror[nm] if nm in wmirror else wdram[nm]) for nm in wdim}
    phasew_gm = C["gm0"]

    xn_tm = [sab(f"xn_tm{i}", [128, 4, 1024], BF16) for i in range(2)]
    xn_fm = [sab(f"xn_fm{i}", [128, 8, 512], BF16) for i in range(2)]
    kst = [sab(f"kst{i}", [128, 512], BF16) for i in range(2)]
    ksq = [sab(f"ksq{i}", [128, 512], BF16) for i in range(2)]
    vstage = [sab(f"vstage{i}", [128, 1040], BF16) for i in range(2)]

    for i in range(2):
        on("vector", "memset", vstage[i][:].rearrange("p (h c) -> p h c", c=65)[:, :, 64:65], 1.0)
    on("vector", "nop", inc="dv", n=1)
    wait("tensor", "dc", dc_total)
    wait("scalar", "dc", dc_total)
    if wmirror:
        wait("tensor", "sg", stage_ph)
        barrier()
        wait("sync", "gm0", C["gm0"])
        dma("sync", xbuf[0][:], x_kv[0:512, :].rearrange("(t p) d -> p t d", p=128), "d0")
        pre_snap[0] = C["d0"]
        dma("sync", xbuf[1][:], x_kv[512:1024, :].rearrange("(t p) d -> p t d", p=128), "d1")
        pre_snap[1] = C["d1"]

    ln_calls = [0]
    pb_last = {0: 0, 1: 0}
    pending_sqrt = [None]
    ln_snb = [0]

    def flush_sqrt():
        if pending_sqrt[0] is not None:
            pending_sqrt[0]()
            pending_sqrt[0] = None

    def emit_ln_stats(xb, xsem, xdma_snap, p):
        st, mvp, tm, rc = stats[p], mv[p], tmp4[p], rec4[p]
        wait("vector", xsem, xdma_snap)
        on("vector", "drain")
        for t in range(4):
            on("vector", "bn_stats", st[:, t * 12:t * 12 + 6], xb[:, t, 0:512])
            on("vector", "bn_stats", st[:, t * 12 + 6:t * 12 + 12], xb[:, t, 512:1024])
        on("vector", "drain")
        for t in range(4):
            on("vector", "bn_aggr", mvp[:, 2 * t:2 * t + 2], st[:, t * 12:t * 12 + 12])
        on("vector", "drain")
        on("vector", "tensor_scalar_add", tm[:], mvp[:].rearrange("p (t s) -> p t s", s=2)[:, :, 1], EPS)
        on("vector", "drain")
        on("vector", "reciprocal", rc[:], tm[:], inc="dv", n=1)
        snap_dv = C["dv"]

        def ln_sqrt_part(snap_dv=snap_dv, p=p):
            wait("scalar", "dv", snap_dv)
            on("scalar", "drain")
            on("scalar", "activation", rstd4[p][:], rec4[p][:], AF.Sqrt, inc="ac", n=1)
            on("scalar", "drain")
            snap_ac = C["ac"]
            wait("vector", "ac", snap_ac)
            on("vector", "scalar_tensor_tensor", nb4[p][:], mv[p][:].rearrange("p (t s) -> p t s", s=2)[:, :, 0], -1.0, rstd4[p][:], ALU.mult, ALU.mult, inc="dv", n=1)
            ln_snb[0] = C["dv"]
        pending_sqrt[0] = ln_sqrt_part
        return None

    def emit_ln_finish(xb, xnt, xn, p):
        flush_sqrt()
        wait("scalar", "dv", ln_snb[0])
        for t in range(4):
            on("scalar", "activation", xnt[:, t, :], xb[:, t, :], AF.Identity,
               scale=rstd4[p][:, t:t + 1], bias=nb4[p][:, t:t + 1],
               inc=("ac" if t == 3 else None), n=(1 if t == 3 else None))
        snap_apply = C["ac"]
        wait("tensor", "ac", snap_apply)
        for t in range(4):
            for half in range(2):
                bank = pb[half]
                wait("tensor", "dv", pb_last[half])
                for c4 in range(4):
                    c = half * 4 + c4
                    on("tensor", "transpose", bank[:, c4 * 128:(c4 + 1) * 128],
                       xnt[:, t, c * 128:(c + 1) * 128], ident[:],
                       inc=("pe" if c4 == 3 else None), n=(1 if c4 == 3 else None))
                snap_tr = C["pe"]
                wait("vector", "pe", snap_tr)
                on("vector", "tensor_copy",
                   xn[:, half * 4:half * 4 + 4, t * 128:(t + 1) * 128],
                   bank[:, 0:512].rearrange("p (c q) -> p c q", c=4),
                   inc="dv", n=1)
                pb_last[half] = C["dv"]
        return C["dv"], snap_apply

    xa_lastuse = {}
    xn_ready = {}
    kst_spill = {}
    ksq_use = {}
    vstage_spill = {}
    vstage_ready = {}
    kn_red = {}
    qn_red = {}
    xn_lastuse = {}

    def emit_kv_body(a, xn, xnrdy):
        wait("tensor", "dv", xnrdy)
        if a == 0:
            wait("tensor", "sg", stage_k)
        for kc in range(8):
            bank = pf[kc % 2]
            if kc >= 2:
                wait("tensor", "ac", ksq_use.get(("evac", a, kc - 2), 0))
            for ic in range(8):
                on("tensor", "matmul", bank[:],
                   wqk_bf[:, ic, 1024 + kc * 128:1024 + (kc + 1) * 128],
                   xn[:, ic, :], start=(ic == 0), stop=(ic == 7),
                   inc=("pe" if ic == 7 else None), n=(1 if ic == 7 else None))
            snap_mm = C["pe"]
            wait("scalar", "pe", snap_mm)
            idx = a * 8 + kc
            if idx >= 2:
                wait("scalar", f"g{kc % 2}", kst_spill[idx - 2])
            on("scalar", "activation", kst[kc % 2][:], bank[:], AF.Identity,
               bias=bqk_c[:, 8 + kc:9 + kc], inc="ac", n=1)
            if idx >= 2:
                wait("scalar", "pe", ksq_use.get(("nmm", a, kc - 2), 0))
            on("scalar", "activation", ksq[kc % 2][:], bank[:], AF.Square,
               bias=bqk_c[:, 8 + kc:9 + kc], inc="ac", n=1)
            ksq_use[("evac", a, kc)] = C["ac"]
            snap_evac = C["ac"]
            wait("gpsimd", "ac", snap_evac - 1)
            dma("gpsimd", k_sp[kc * 128:(kc + 1) * 128, a * 512:(a + 1) * 512], kst[kc % 2][:], f"g{kc % 2}")
            kst_spill[idx] = C[f"g{kc % 2}"]

            def norm_mm(kcn, idxn):
                wait("tensor", "ac", ksq_use[("evac", a, kcn)])
                if idxn >= 1:
                    wait("tensor", "dv", kn_red[idxn - 1])
                on("tensor", "matmul", pf[2][:], ohm[:], ksq[kcn % 2][:], start=True, stop=True, inc="pe", n=1)
                ksq_use[("nmm", a, kcn)] = C["pe"]
                wait("vector", "pe", C["pe"])
                on("vector", "tensor_reduce", kn_part[0:2, kcn * 8 + a: kcn * 8 + a + 1], pf[2][0:2, :], AX.X, ALU.max, inc="dv", n=1)
                kn_red[idxn] = C["dv"]
            if kc == 3:
                flush_sqrt()
            if kc >= 1:
                norm_mm(kc - 1, idx - 1)
            if kc == 7:
                norm_mm(7, idx)
        if a == 0:
            wait("tensor", "sg", stage_v)
        for t in range(4):
            banks = (pf[4], pf[5]) if t % 2 == 0 else (pf[0], pf[1])
            vidx = a * 4 + t
            if t % 2 == 1:
                wait("tensor", "ac", ksq_use[("evac", a, 7)])
            if vidx >= 2:
                wait("tensor", "dv", vstage_ready[vidx - 2])
            for half in range(2):
                bank = banks[half]
                on("tensor", "matmul", bank[:], e0m[:], bv_pad[:, half * 512:(half + 1) * 512], start=True, stop=False)
                for ic in range(8):
                    on("tensor", "matmul", bank[:],
                       xn[:, ic, t * 128:(t + 1) * 128],
                       wv_bf[:, ic, half * 512:(half + 1) * 512],
                       start=False, stop=(ic == 7),
                       inc=("pe" if ic == 7 else None), n=(1 if ic == 7 else None))
            snap_vmm = C["pe"]
            wait("vector", "pe", snap_vmm)
            if vidx >= 2:
                wait("vector", f"g{2 + t % 2}", vstage_spill[vidx - 2])
            for half in range(2):
                on("vector", "tensor_copy",
                   vstage[t % 2][:].rearrange("p (h c) -> p h c", c=65)[:, half * 8:half * 8 + 8, 0:64],
                   banks[half][:].rearrange("p (h d) -> p h d", d=64),
                   inc=("dv" if half == 1 else None), n=(1 if half == 1 else None))
            vstage_ready[vidx] = C["dv"]
            wait("gpsimd", "dv", vstage_ready[vidx])
            dma("gpsimd", v_sp[a * 512 + t * 128: a * 512 + (t + 1) * 128, :], vstage[t % 2][:], f"g{2 + t % 2}")
            vstage_spill[vidx] = C[f"g{2 + t % 2}"]

    def emit_q_body(b, xn, xnrdy):
        wait("tensor", "dv", xnrdy)
        if b == 0:
            wait("tensor", "sg", stage_q)
        for qc in range(8):
            bank = pf[qc % 2]
            idx = b * 8 + qc
            if qc >= 2:
                wait("tensor", "ac", ksq_use.get(("qevac", b, qc - 2), 0))
            elif b == 0 and qc < 2:
                wait("tensor", "dv", vstage_ready[NBLK_A * 4 - 1])
            for ic in range(8):
                on("tensor", "matmul", bank[:],
                   wqk_bf[:, ic, qc * 128:(qc + 1) * 128],
                   xn[:, ic, :], start=(ic == 0), stop=(ic == 7),
                   inc=("pe" if ic == 7 else None), n=(1 if ic == 7 else None))
            snap_mm = C["pe"]
            wait("scalar", "pe", snap_mm)
            on("scalar", "activation", q_fm[:, qc, b * 512:(b + 1) * 512], bank[:],
               AF.Identity, bias=bqk_c[:, qc:qc + 1], inc="ac", n=1)
            if idx >= 2:
                wait("scalar", "pe", ksq_use.get(("qnmm", b, qc - 2), 0))
            on("scalar", "activation", ksq[qc % 2][:], bank[:], AF.Square,
               bias=bqk_c[:, qc:qc + 1], inc="ac", n=1)
            ksq_use[("qevac", b, qc)] = C["ac"]

            def qnorm_mm(qcn, idxn):
                wait("tensor", "ac", ksq_use[("qevac", b, qcn)])
                if idxn >= 1:
                    wait("tensor", "dv", qn_red[idxn - 1])
                on("tensor", "matmul", pf[2][:], ohm[:], ksq[qcn % 2][:], start=True, stop=True, inc="pe", n=1)
                ksq_use[("qnmm", b, qcn)] = C["pe"]
                wait("vector", "pe", C["pe"])
                on("vector", "tensor_reduce", qn_part[0:2, qcn * 4 + b: qcn * 4 + b + 1], pf[2][0:2, :], AX.X, ALU.max, inc="dv", n=1)
                qn_red[idxn] = C["dv"]
            if qc == 3:
                flush_sqrt()
            if qc >= 1:
                qnorm_mm(qc - 1, idx - 1)
            if qc == 7:
                qnorm_mm(7, idx)

    NALL = NBLK_A + NBLK_B
    binfo = [("A", i) for i in range(NBLK_A)] + [("B", i) for i in range(NBLK_B)]
    snb = {}
    for i in range(NALL + 1):
        if i < NALL:
            ph, idx = binfo[i]
            xb = xbuf[i % 2]
            if i >= 2:
                wait("sync", "ac", xa_lastuse[i - 2])
                src_ap = (x_kv[idx * 512:(idx + 1) * 512, :] if ph == "A"
                          else x_q[idx * 512:(idx + 1) * 512, :])
                dma("sync", xb[:], src_ap.rearrange("(t p) d -> p t d", p=128), f"d{i % 2}")
                snap_x = C[f"d{i % 2}"]
            else:
                snap_x = pre_snap[i]
            snb[i] = emit_ln_stats(xb, f"d{i % 2}", snap_x, i % 2)
        if i >= 1:
            ph1, idx1 = binfo[i - 1]
            if ph1 == "A":
                emit_kv_body(idx1, xn_fm[(i - 1) % 2], xn_ready[i - 1])
            else:
                emit_q_body(idx1, xn_fm[(i - 1) % 2], xn_ready[i - 1])
        if i < NALL:
            xn_ready[i], xa_lastuse[i] = emit_ln_finish(xbuf[i % 2], xn_tm[i % 2], xn_fm[i % 2], i % 2)
    barrier()
    es_ab.close()
    es_ps.close()

    es_c = ExitStack()
    scb = lambda name, shape, dt: es_c.enter_context(nc.sbuf_tensor(name, shape, dt))
    psc = [es_c.enter_context(nc.psum_tensor(f"psc{i}", [128, 1024], F32)) for i in range(2)]
    pav = [es_c.enter_context(nc.psum_tensor(f"pav{i}", [128, 512], F32)) for i in range(4)]
    vb = [scb(f"vb{i}", [128, 32, 130], BF16) for i in range(3)]
    kb = [scb(f"kb{i}", [128, 4096], BF16) for i in range(3)]
    ptb = scb("ptb", [128, 8, 1024], BF16)

    for gs in ("g0", "g1", "g2", "g3"):
        wait("scalar", gs, C[gs])
    wait("tensor", "dg", gm_snap)
    on("vector", "tensor_reduce", kmax[:], kn_part[:].rearrange("p (h a) -> p h a", a=8), AX.X, ALU.max)
    on("vector", "drain")
    for g_ in range(NGRP):
        on("vector", "tensor_mul", m2[:, g_ * 8:(g_ + 1) * 8],
           qn_part[:].rearrange("p (h g) -> p h g", g=4)[:, :, g_], kmax[:])
    on("vector", "drain")
    on("vector", "nop", inc="dv", n=1)
    wait("scalar", "dv", C["dv"])
    on("scalar", "activation", msm[:], m2[:], AF.Sqrt, scale=1.0 / 64.0, inc="ac", n=1)
    wait("tensor", "ac", C["ac"])
    on("tensor", "matmul", psc[0][:, 0:32], sel0[:], msm[:], start=True, stop=True)
    on("tensor", "matmul", psc[0][:, 32:64], sel1[:], msm[:], start=True, stop=True, inc="pe", n=1)
    wait("scalar", "pe", C["pe"])
    on("scalar", "activation", mb[:], psc[0][:, 0:64], AF.Identity, scale=-1.0, inc="ac", n=1)
    on("scalar", "drain")
    snapm = C["ac"]
    wait("vector", "ac", snapm)
    on("vector", "tensor_tensor", mbp[:], mb[:, 0:32], mb[:, 32:64], ALU.min, inc="dv", n=1)
    wait("scalar", "dv", C["dv"])
    mb_ready = C["ac"]

    kb_lastuse = {}
    vb_lastuse = {}
    pending_tail = [None]
    last_ao = [0]
    kdma_snap = {}
    vdma_snap = {}
    av_snap = {}
    KSEM = ["d4", "d5", "d2"]
    VSEM = ["d6", "d7", "d3"]
    semA_last = {0: 0, 1: 0, 2: 0}
    semB_last = {0: 0, 1: 0, 2: 0}

    def emit_kv_load(n):
        if n >= NGRP * 8:
            return
        gl, hl = n // 8, n % 8
        El = 8 * gl + 8
        KEl = El * 128
        wait("sync", "pe", semA_last[n % 3])
        wait("sync", f"g{hl % 2}", kst_spill[(2 * gl + 1) * 8 + hl])
        dma("sync", kb[n % 3][:, 0:KEl], k_sp[hl * 128:(hl + 1) * 128, 0:KEl], KSEM[n % 3])
        kdma_snap[n] = C[KSEM[n % 3]]
        wait("sync", "pe", semB_last[n % 3])
        wait("sync", "g2", vstage_spill[(2 * gl + 1) * 4 + 2])
        wait("sync", "g3", vstage_spill[(2 * gl + 1) * 4 + 3])
        dma("sync", vb[n % 3][:, 0:El, :], v_sp[0:KEl, 130 * hl:130 * (hl + 1)].rearrange("(e p) c -> p e c", p=128), VSEM[n % 3])
        vdma_snap[n] = C[VSEM[n % 3]]
    exp_done = {}
    av_done = {}
    stt_done = {}
    x2b_spill = {}
    memset_done = {}

    emit_kv_load(0)
    emit_kv_load(1)
    snb_d = {}
    xn_rdy_d = {}
    out_dma = {}

    def emit_d_load_stats(b):
        xb = xbuf[(b + 1) % 2]
        if b == 0:
            wait("sync", "g4", 32)
        if b == 1:
            wait("sync", "dv", stt_done[(3, 7, 1)])
        wait("sync", f"g{4 + b % 2}", 16 * (b // 2 + 1))
        if b >= 2:
            wait("sync", f"g{6 + b % 2}", out_dma[b - 2])
        dma("sync", xb[:], x2_sp[b * 512:(b + 1) * 512, :].rearrange("(t p) d -> p t d", p=128), f"d{(b + 1) % 2}")
        snap_x = C[f"d{(b + 1) % 2}"]
        snb_d[b] = emit_ln_stats(xb, f"d{(b + 1) % 2}", snap_x, b % 2)

    for g in range(NGRP):
        if g == 3:
            emit_d_load_stats(0)
        E = 8 * g + 8
        KE = E * 128
        xq = xbuf[0]
        if g >= 1:
            wait("sync", "dv", stt_done[(g - 1, 7, 1)])
        dma("sync", xq[:], x_q[g * 512:(g + 1) * 512, :].rearrange("(t p) d -> p t d", p=128), "d0")
        xq_snap = C["d0"]
        x2b = xbuf[1 + g % 2]
        if g >= 2:
            wait("vector", f"g{4 + g % 2}", x2b_spill[g - 2])
        for hp in range(8):
            h0, h1 = 2 * hp, 2 * hp + 1
            if (g, hp) != (0, 0):
                pass
            n_lin = g * 8 + hp
            emit_kv_load(n_lin + 2)
            kdma = kdma_snap[n_lin]
            vdma = vdma_snap[n_lin]
            wait("tensor", KSEM[(g * 8 + hp) % 3], kdma)
            wait("tensor", "ac", mb_ready)
            if g >= 1 or hp >= 1:
                prevpair = (g, hp - 1) if hp >= 1 else (g - 1, 7)
                wait("scalar", "pe", av_done.get(prevpair, 0))
            prevpair = (g, hp - 1) if hp >= 1 else (g - 1, 7)
            wait("tensor", VSEM[(g * 8 + hp) % 3], vdma)

            def av_mms(c, stop):
                md = memset_done.get((g, hp, c))
                if md:
                    wait("tensor", "dv", md)
                on("tensor", "matmul", pav[0][0:65, :], vb[(g * 8 + hp) % 3][:, c, 0:65], ptb[:, c % 8, 0:512], start=(c == 0), stop=stop)
                on("tensor", "matmul", pav[1][0:65, :], vb[(g * 8 + hp) % 3][:, c, 65:130], ptb[:, c % 8, 512:1024], start=(c == 0), stop=stop,
                   inc="pe", n=1)
                av_snap[(g, hp, c)] = C["pe"]

            for c in range(E):
                bank = psc[c % 2]
                bA = bank[:, 0:512]
                bB = bank[:, 512:1024]
                if c >= 2:
                    wait("tensor", "ac", exp_done[(g, hp, c - 2)])
                elif g > 0 or hp > 0:
                    wait("tensor", "ac", exp_done[(prevpair[0], prevpair[1], 8 * prevpair[0] + 8 - 1 - (1 - c))])
                masked = c >= 8 * g
                rel = c - 8 * g
                aw = 128 * (rel // 2) if masked else 0
                bw = aw + 128
                kbt = kb[(g * 8 + hp) % 3]
                if masked:
                    for rt, bb in ((0, bA), (64, bB)):
                        if bw < 512:
                            on("tensor", "matmul", bb[:, bw:512], kbt[rt:rt + 64, c * 128:(c + 1) * 128],
                               q_fm[rt:rt + 64, hp, g * 512 + bw:(g + 1) * 512], start=True, stop=True)
                        on("tensor", "matmul", bb[:, aw:bw], kbt[rt:rt + 64, c * 128:(c + 1) * 128],
                           q_fm[rt:rt + 64, hp, g * 512 + aw:g * 512 + bw], start=True, stop=False)
                    on("tensor", "matmul", bA[:, aw:bw], hm[:], gm[:, rel, aw:bw], start=False, stop=True)
                    on("tensor", "matmul", bB[:, aw:bw], hm[:], gm[:, rel, aw:bw], start=False, stop=True, inc="pe", n=1)
                else:
                    on("tensor", "matmul", bA, kbt[0:64, c * 128:(c + 1) * 128],
                       q_fm[0:64, hp, g * 512:(g + 1) * 512], start=True, stop=True)
                    on("tensor", "matmul", bB, kbt[64:128, c * 128:(c + 1) * 128],
                       q_fm[64:128, hp, g * 512:(g + 1) * 512], start=True, stop=True,
                       inc="pe", n=1)
                snap_s = C["pe"]
                wait("scalar", "pe", snap_s)
                if c >= 8:
                    wait("scalar", "pe", av_snap[(g, hp, c - 8)])
                if masked and aw > 0:
                    wait("vector", "pe", av_snap[(g, hp, c - 8)] if c >= 8 else av_done.get(prevpair, 0))
                    on("vector", "memset",
                       ptb[:, c % 8, :].rearrange("p (h q) -> p h q", h=2)[:, :, 0:aw], 0.0,
                       inc="dv", n=1)
                    memset_done[(g, hp, c)] = C["dv"]
                    on("scalar", "activation",
                       ptb[:, c % 8, :].rearrange("p (h q) -> p h q", h=2)[:, :, aw:512],
                       bank[:].rearrange("p (h q) -> p h q", h=2)[:, :, aw:512],
                       AF.Exp, scale=0.125, bias=mbp[:, g * 8 + hp:g * 8 + hp + 1], inc="ac", n=1)
                else:
                    on("scalar", "activation", ptb[:, c % 8, :], bank[:], AF.Exp, scale=0.125,
                       bias=mbp[:, g * 8 + hp:g * 8 + hp + 1], inc="ac", n=1)
                exp_done[(g, hp, c)] = C["ac"]
                if c == 3 and pending_tail[0] is not None:
                    pending_tail[0]()
                    pending_tail[0] = None
                if c >= 2:
                    if c == 2 and (g > 0 or hp > 0):
                        wait("tensor", "dv", last_ao[0])
                    wait("tensor", "ac", exp_done[(g, hp, c - 2)])
                    av_mms(c - 2, False)
            wait("tensor", "ac", exp_done[(g, hp, E - 2)])
            av_mms(E - 2, False)
            wait("tensor", "ac", exp_done[(g, hp, E - 1)])
            av_mms(E - 1, True)
            av_done[(g, hp)] = C["pe"]
            semA_last[(g * 8 + hp) % 3] = C["pe"]
            semB_last[(g * 8 + hp) % 3] = C["pe"]
            wait("vector", "pe", av_done[(g, hp)])
            for pi in range(2):
                on("vector", "tensor_copy", ao[pi][:], pav[pi][0:65, :],
                   inc=("dv" if pi == 1 else None), n=(1 if pi == 1 else None))
            ao_snap = C["dv"]

            def attn_tail(g=g, hp=hp, ao_snap=ao_snap, xq=xq, x2b=x2b, xq_snap=xq_snap):
                for pi in range(2):
                    h = 2 * hp + pi
                    psT = pav[2 + pi]
                    wait("tensor", "dv", ao_snap)
                    if (g, hp) != (0, 0):
                        pv = (g, hp - 1) if hp >= 1 else (g - 1, 7)
                        wait("tensor", "dv", stt_done.get((pv[0], pv[1], pi), 0))
                    for t in range(4):
                        on("tensor", "transpose", psT[:, t * 65:t * 65 + 65], ao[pi][0:65, t * 128:(t + 1) * 128], identf[0:65, 0:65],
                           inc=("pe" if t == 3 else None), n=(1 if t == 3 else None))
                    snap_tr = C["pe"]
                    wait("vector", "pe", snap_tr)
                    wait("vector", "d0", xq_snap)
                    on("vector", "tensor_copy", zc[:], psT[:, 0:260].rearrange("p (t c) -> p t c", c=65)[:, :, 64])
                    on("vector", "drain")
                    on("vector", "reciprocal", zr[:], zc[:])
                    on("vector", "drain")
                    for t in range(4):
                        on("vector", "scalar_tensor_tensor", x2b[:, t, h * 64:(h + 1) * 64],
                           psT[:, t * 65:t * 65 + 64], zr[:, t:t + 1], xq[:, t, h * 64:(h + 1) * 64],
                           ALU.mult, ALU.add, inc=("dv" if t == 3 else None), n=(1 if t == 3 else None))
                    stt_done[(g, hp, pi)] = C["dv"]
            pending_tail[0] = attn_tail
            last_ao[0] = ao_snap
        if pending_tail[0] is not None:
            pending_tail[0]()
            pending_tail[0] = None
        wait("gpsimd", "dv", stt_done[(g, 7, 1)])
        dma("gpsimd", x2_sp[g * 512:(g + 1) * 512, :].rearrange("(t p) d -> p t d", p=128), x2b[:], f"g{4 + g % 2}")
        x2b_spill[g] = C[f"g{4 + g % 2}"]
    barrier()
    es_c.close()
    es_q.close()

    es_d = ExitStack()
    sdb = lambda name, shape, dt: es_d.enter_context(nc.sbuf_tensor(name, shape, dt))
    xn2_tm = [sdb(f"xn2tm{i}", [128, 4, 1024], BF16) for i in range(2)]
    xn2_fm = [sdb(f"xn2fm{i}", [128, 8, 512], BF16) for i in range(2)]
    wb12 = [sdb(f"wb12_{i}", [128, 2, 8, 128], BF16) for i in range(3)]
    wbh = [sdb(f"wbh{i}", [128, 32, 128], BF16) for i in range(3)]
    su = [sdb(f"su{i}", [128, 512], BF16) for i in range(2)]
    sv2 = [sdb(f"sv2_{i}", [128, 512], BF16) for i in range(2)]
    abuf = sdb("abuf", [128, 32, 512], BF16)
    bbuf = sdb("bbuf", [128, 32, 512], BF16)
    ytmp = [sdb(f"ytmp{i}", [128, 512], F32) for i in range(2)]

    x2_lastuse = {}
    xn2_lastuse = {}
    ev_done = {}
    mul_done = {}
    w_lastuse = {}
    y_add_done = {}

    if wmirror:
        wait("sync", "gm0", phasew_gm)

    def emit_ffn_body(b):
        xb = xbuf[(b + 1) % 2]
        xn = xn2_fm[b % 2]
        wait("tensor", "dv", xn_rdy_d[b])
        for oc in range(32):
            widx = b * 32 + oc
            wbuf = widx % 3
            wait("sync", "pe", semA_last[wbuf])
            dma("sync", wb12[wbuf][:, 0, :, :], wsrc["w1"][oc], KSEM[wbuf])
            dma("sync", wb12[wbuf][:, 1, :, :], wsrc["w2"][oc], KSEM[wbuf])
            wdma = C[KSEM[wbuf]]
            bA = pf[2 * (oc % 2)]
            bB = pf[2 * (oc % 2) + 1]
            wait("tensor", KSEM[wbuf], wdma)
            if oc >= 2:
                wait("tensor", "ac", ev_done.get(("w12", widx - 2), 0))
            elif b >= 1:
                wait("tensor", "ac", ev_done[("w3", b * 8 - 1)])
                wait("tensor", "dv", y_add_done[("add", b * 8 - 1)])
            for ic in range(8):
                on("tensor", "matmul", bA[:], wb12[wbuf][:, 0, ic, :], xn[:, ic, :], start=(ic == 0), stop=(ic == 7))
            for ic in range(8):
                on("tensor", "matmul", bB[:], wb12[wbuf][:, 1, ic, :], xn[:, ic, :], start=(ic == 0), stop=(ic == 7),
                   inc=("pe" if ic == 7 else None), n=(1 if ic == 7 else None))
            w_lastuse[("w12", widx)] = C["pe"]
            semA_last[wbuf] = C["pe"]
            snap_mm = C["pe"]
            wait("scalar", "pe", snap_mm)
            if oc >= 2 or b >= 1:
                wait("scalar", "dv", mul_done.get(widx - 2, 0))
            on("scalar", "activation", su[oc % 2][:], bA[:], AF.Silu, scale=scc["s1"][:, oc:oc + 1])
            on("scalar", "activation", sv2[oc % 2][:], bB[:], AF.Identity, scale=scc["s2"][:, oc:oc + 1], inc="ac", n=1)
            ev_done[("w12", widx)] = C["ac"]
            wait("vector", "ac", C["ac"])
            on("vector", "tensor_mul", abuf[:, oc, :], su[oc % 2][:], sv2[oc % 2][:], inc="dv", n=1)
            mul_done[widx] = C["dv"]
            if oc == 3:
                flush_sqrt()
        wait("tensor", "dv", mul_done[b * 32 + 31])
        for oc in range(32):
            widx = b * 32 + oc
            wbuf = widx % 3
            wait("sync", "pe", semB_last[wbuf])
            dma("sync", wbh[wbuf][:], wsrc["h1"][oc], VSEM[wbuf])
            wdma = C[VSEM[wbuf]]
            bank = pf[oc % 2]
            wait("tensor", VSEM[wbuf], wdma)
            if oc >= 2 or b >= 1:
                wait("tensor", "ac", ev_done.get(("h1", widx - 2), 0))
            elif b == 0:
                wait("tensor", "ac", ev_done[("w12", 31)])
            for ic in range(32):
                on("tensor", "matmul", bank[:], wbh[wbuf][:, ic, :], abuf[:, ic, :], start=(ic == 0), stop=(ic == 31),
                   inc=("pe" if ic == 31 else None), n=(1 if ic == 31 else None))
            w_lastuse[("h1", widx)] = C["pe"]
            semB_last[wbuf] = C["pe"]
            wait("scalar", "pe", C["pe"])
            on("scalar", "activation", bbuf[:, oc, :], bank[:], AF.Gelu, scale=scc["sh1"][:, oc:oc + 1], inc="ac", n=1)
            ev_done[("h1", widx)] = C["ac"]
        wait("tensor", "ac", ev_done[("h1", b * 32 + 31)])
        for oc in range(32):
            widx = b * 32 + oc
            wbuf = widx % 3
            wait("sync", "pe", semB_last[wbuf])
            dma("sync", wbh[wbuf][:], wsrc["h2"][oc], VSEM[wbuf])
            wdma = C[VSEM[wbuf]]
            bank = pf[oc % 2]
            wait("tensor", VSEM[wbuf], wdma)
            if oc >= 2 or b >= 1:
                wait("tensor", "ac", ev_done.get(("h2", widx - 2), 0))
            for ic in range(32):
                on("tensor", "matmul", bank[:], wbh[wbuf][:, ic, :], bbuf[:, ic, :], start=(ic == 0), stop=(ic == 31),
                   inc=("pe" if ic == 31 else None), n=(1 if ic == 31 else None))
            w_lastuse[("h2", widx)] = C["pe"]
            semB_last[wbuf] = C["pe"]
            wait("scalar", "pe", C["pe"])
            on("scalar", "activation", abuf[:, oc, :], bank[:], AF.Gelu, scale=scc["sh2"][:, oc:oc + 1], inc="ac", n=1)
            ev_done[("h2", widx)] = C["ac"]
        wait("tensor", "ac", ev_done[("h2", b * 32 + 31)])
        for oc in range(8):
            widx = b * 8 + oc
            wbuf = (b * 8 + oc) % 3
            wait("sync", "pe", semB_last[wbuf])
            dma("sync", wbh[wbuf][:], wsrc["w3"][oc], VSEM[wbuf])
            wdma = C[VSEM[wbuf]]
            bank = pf[oc % 2]
            wait("tensor", VSEM[wbuf], wdma)
            if oc >= 2 or b >= 1:
                wait("tensor", "ac", ev_done.get(("w3", widx - 2), 0))
            for ic in range(32):
                on("tensor", "matmul", bank[:], wbh[wbuf][:, ic, :], abuf[:, ic, :], start=(ic == 0), stop=(ic == 31),
                   inc=("pe" if ic == 31 else None), n=(1 if ic == 31 else None))
            w_lastuse[("w3", widx)] = C["pe"]
            semB_last[wbuf] = C["pe"]
            wait("scalar", "pe", C["pe"])
            if oc >= 2 or b >= 1:
                wait("scalar", "pe", y_add_done.get(("tr", widx - 2), 0))
            on("scalar", "activation", ytmp[oc % 2][:], bank[:], AF.Identity, scale=scc["s3"][:, oc:oc + 1], inc="ac", n=1)
            ev_done[("w3", widx)] = C["ac"]

            def y_chain(ocn, widxn):
                yb = pf[4 + ocn % 2]
                wait("tensor", "ac", ev_done[("w3", widxn)])
                if ocn >= 2 or b >= 1:
                    wait("tensor", "dv", y_add_done.get(("add", widxn - 2), 0))
                for t in range(4):
                    on("tensor", "transpose", yb[:, t * 128:(t + 1) * 128], ytmp[ocn % 2][:, t * 128:(t + 1) * 128], identf[:],
                       inc=("pe" if t == 3 else None), n=(1 if t == 3 else None))
                y_add_done[("tr", widxn)] = C["pe"]
                wait("vector", "pe", C["pe"])
                for t in range(4):
                    on("vector", "tensor_tensor", xb[:, t, ocn * 128:(ocn + 1) * 128], yb[:, t * 128:(t + 1) * 128],
                       xb[:, t, ocn * 128:(ocn + 1) * 128], ALU.add,
                       inc=("dv" if t == 3 else None), n=(1 if t == 3 else None))
                y_add_done[("add", widxn)] = C["dv"]
            if oc >= 1:
                y_chain(oc - 1, widx - 1)
            if oc == 7:
                y_chain(7, widx)
        xn2_lastuse[b] = w_lastuse[("w12", b * 32 + 31)]
        x2_lastuse[b] = C["dv"]
        wait("gpsimd", "dv", y_add_done[("add", b * 8 + 7)])
        for t in range(4):
            dma("gpsimd", out_d[b * 512 + t * 128: b * 512 + (t + 1) * 128, :], xb[:, t, :], f"g{6 + b % 2}")
        out_dma[b] = C[f"g{6 + b % 2}"]

    for i in range(NBLK_D + 1):
        if 0 < i < NBLK_D:
            emit_d_load_stats(i)
        if i >= 1:
            emit_ffn_body(i - 1)
        if i < NBLK_D:
            xn_rdy_d[i], _apl = emit_ln_finish(xbuf[(i + 1) % 2], xn2_tm[i % 2], xn2_fm[i % 2], i % 2)

    wait("sync", "g6", C["g6"])
    wait("sync", "g7", C["g7"])
    wait("gpsimd", "g6", C["g6"])
    wait("gpsimd", "g7", C["g7"])

    with nc.Block() as blk:
        @blk.sync
        def _(g):
            for f in EV["sync"]:
                f(g)

        @blk.gpsimd
        def _(g):
            for f in EV["gpsimd"]:
                f(g)

        @blk.tensor
        def _(g):
            for f in EV["tensor"]:
                f(g)

        @blk.vector
        def _(g):
            for f in EV["vector"]:
                f(g)

        @blk.scalar
        def _(g):
            for f in EV["scalar"]:
                f(g)

    es_d.close()
    es.close()
    return nc



def _tile_w(w):
    o, i = w.shape
    wt = np.ascontiguousarray(w.T)
    t = wt.reshape(i // 128, 128, o // 128, 128)
    return np.ascontiguousarray(t.transpose(2, 1, 0, 3))


def _maybe_bf16(a):
    ab = a.astype(ml_dtypes.bfloat16)
    if (ab.astype(np.float32) == a).all():
        return ab, True
    return a, False


def _host_prep(inp):
    x = np.asarray(inp["L0_attn_hidden_states"], np.float32)
    wqk = np.asarray(inp["L0_attn_Wqk"], np.float32)
    bqk = np.asarray(inp["L0_attn_bqk"], np.float32)
    wv = np.asarray(inp["L0_attn_Wv"], np.float32)
    bv = np.asarray(inp["L0_attn_bv"], np.float32)
    ws = {}
    flags = {}
    for nm, key in (("w1", "L0_ffn_w1_w"), ("w2", "L0_ffn_w2_w"), ("h1", "L0_ffn_h1_w"),
                    ("h2", "L0_ffn_h2_w"), ("w3", "L0_ffn_w3_w")):
        t = _tile_w(np.asarray(inp[key], np.float32))
        t2, ok = _maybe_bf16(t)
        ws[nm] = t2
        flags[nm] = ok
    scales = {"s1": inp["L0_ffn_w1_s"], "s2": inp["L0_ffn_w2_s"], "sh1": inp["L0_ffn_h1_s"],
              "sh2": inp["L0_ffn_h2_s"], "s3": inp["L0_ffn_w3_s"]}

    ident = np.eye(128, dtype=ml_dtypes.bfloat16)
    identf = np.eye(128, dtype=np.float32)
    d = np.arange(128)
    hmat = (d[None, :] >= d[:, None]).astype(ml_dtypes.bfloat16)

    def build_g(cp):
        gmk = np.zeros((8, 128, 512), np.float32)
        for r in range(8):
            for t in range(4):
                dstate = 2 * t + cp
                if r == dstate:
                    for qr in range(127):
                        gmk[r, qr + 1, 128 * t + qr] = MASKB
                elif r > dstate:
                    gmk[r, 0, 128 * t:128 * t + 128] = MASKB
        return gmk.astype(ml_dtypes.bfloat16)

    gms = [build_g(0), build_g(1)]
    in_maps = []
    for core in range(8):
        b, cp = core >> 1, core & 1
        xb = x[b]
        xq = np.ascontiguousarray(xb.reshape(32, 128, D)[cp::2].reshape(2048, D))
        m = {
            "x_kv": xb, "x_q": xq,
            "wqk": wqk, "bqk": np.ascontiguousarray(bqk.reshape(16, 128).T),
            "wv": wv, "bv": bv.reshape(1, D),
            "ident": ident, "identf": identf, "hm": hmat, "gm": gms[cp],
        }
        for nm in ws:
            m["w_" + nm] = ws[nm]
        for nm in ("s1", "s2", "sh1", "sh2", "s3"):
            sc = np.asarray(scales[nm], np.float32)
            m[nm] = np.ascontiguousarray(sc.reshape(-1, 128).T)
        in_maps.append(m)
    return flags, in_maps


def kernel(**inp):
    flags, in_maps = _host_prep(inp)
    nc = _build(flags)
    res = run_bass_kernel_spmd(nc, in_maps, core_ids=list(range(8)))
    out = np.empty((B, S, D), np.float32)
    for core in range(8):
        b, cp = core >> 1, core & 1
        oc = res.results[core]["out"].reshape(16, 128, D)
        out.reshape(B, 32, 128, D)[b, cp::2] = oc
    return out
